# version 16
# baseline (speedup 1.0000x reference)
import numpy as np
import ml_dtypes
import concourse.bass as bass
import concourse.mybir as mybir
from concourse.bass_utils import run_bass_kernel_spmd

F32 = mybir.dt.float32
BF16 = mybir.dt.bfloat16
AF = mybir.ActivationFunctionType
ALU = mybir.AluOpType

D = 1024
NCK = 8
T = 512
NT = 8
NSUB = 4
NHC = 32
NH = 16
HD = 64
EPS = 1e-5
ENGINES = ("tensor", "vector", "scalar", "sync", "gpsimd")

DBG = False


class Em:

    def __init__(self, nc):
        self.nc = nc
        self.q = {e: [] for e in ENGINES}
        self.sems = {}
        self.watermark = {}

    def sem(self, name):
        self.sems[name] = [self.nc.alloc_semaphore(name), 0]

    AUTO = {"vector": "sV", "scalar": "sA"}

    def op(self, eng, fn, waits=(), inc=None, n=1):
        ws = [w for w in waits if w is not None and w[1] > 0]
        auto = self.AUTO.get(eng)
        if auto is not None:
            prev = self.sems[auto][1]
            if prev > 0:
                ws.append((auto, prev))
            inc = auto
            n = 1
        self.q[eng].append((ws, fn, inc, n))
        if inc is not None:
            self.sems[inc][1] += n
            return (inc, self.sems[inc][1])
        return None

    def retro_inc(self, eng, sem, n=1):
        ws, fn, inc, ni = self.q[eng][-1]
        assert inc is None, "last op already increments a semaphore"
        self.q[eng][-1] = (ws, fn, sem, n)
        self.sems[sem][1] += n
        return (sem, self.sems[sem][1])

    def dma(self, eng, fn, waits=(), inc=None):
        return self.op(eng, fn, list(waits) + [self.tok(inc)], inc, 16)

    def tok(self, name):
        return (name, self.sems[name][1])

    def flush(self, name):
        qs, self.q = self.q, {e: [] for e in ENGINES}
        nc = self.nc

        def mk(eng_name):
            def body(eng):
                for ws, fn, inc, n in qs[eng_name]:
                    need = []
                    for sname, val in ws:
                        key = (eng_name, sname)
                        if self.watermark.get(key, 0) < val:
                            need.append((sname, val))
                            self.watermark[key] = val
                    for sname, val in need[:-1]:
                        eng.wait_ge(self.sems[sname][0], val)
                    inst = fn(eng)
                    if need:
                        sname, val = need[-1]
                        inst._wait_ge(self.sems[sname][0], val)
                    if inc is not None:
                        inst.then_inc(self.sems[inc][0], n)
            return body

        with nc.Block(name) as block:
            for e in ENGINES:
                if qs[e]:
                    getattr(block, e)(mk(e))


class Buf:
    def __init__(self):
        self.writer = None
        self.readers = []

    def read(self, tok):
        if tok is not None:
            self.readers.append(tok)

    def write_waits(self):
        r, self.readers = self.readers, []
        return r

    def wrote(self, tok):
        self.writer = tok


def build_nc(dbg=False, p2_level=5, p1_tiles=NT, p2_tiles=NT, parts='ABCD'):
    nc = bass.Bass("TRN2", target_bir_lowering=False)

    xT = nc.dram_tensor("xT", [D, NT * T], F32, kind="ExternalInput").ap()
    wq_d = nc.dram_tensor("wq", [D, D], F32, kind="ExternalInput").ap()
    wk_d = nc.dram_tensor("wk", [D, D], F32, kind="ExternalInput").ap()
    wv_d = nc.dram_tensor("wv", [D, D], F32, kind="ExternalInput").ap()
    wu_d = nc.dram_tensor("wu", [D, D], F32, kind="ExternalInput").ap()
    wo_d = nc.dram_tensor("wo", [D, D], F32, kind="ExternalInput").ap()
    w1tb = nc.dram_tensor("w1tb", [8, 128, 4096], F32, kind="ExternalInput").ap()
    w2tb = nc.dram_tensor("w2tb", [8, 128, 4096], F32, kind="ExternalInput").ap()
    cstf_d = nc.dram_tensor("cst_f", [128, 83], F32, kind="ExternalInput").ap()
    cstb_d = nc.dram_tensor("cst_b", [128, 401], BF16, kind="ExternalInput").ap()
    cstr_d = nc.dram_tensor("cst_r", [1, 2048], F32, kind="ExternalInput").ap()

    outT = nc.dram_tensor("outT", [D, NT * T], F32, kind="ExternalOutput").ap()
    w1c = nc.dram_tensor("w1c", [8, 128, 4096], BF16, kind="Internal").ap()
    w2c = nc.dram_tensor("w2c", [8, 128, 4096], BF16, kind="Internal").ap()
    w1f = nc.dram_tensor("w1f", [8, 128, 4096], BF16, kind="Internal").ap()
    w2f = nc.dram_tensor("w2f", [8, 128, 4096], BF16, kind="Internal").ap()
    lnx = nc.dram_tensor("lnx", [NT, 128, NCK * T], BF16, kind="Internal").ap()

    dbg_out = {}
    if dbg:
        for nm, shape in [("dbg_ln1", [128, D]), ("dbg_kv", [128, 512]),
                          ("dbg_q", [128, D]), ("dbg_attn", [128, D]),
                          ("dbg_x1", [128, D]), ("dbg_ln3", [128, D]),
                          ("dbg_h", [128, 512]), ("dbg_t", [4, T]),
                          ("dbg_u2", [16, T])]:
            dbg_out[nm] = nc.dram_tensor(nm, shape, F32, kind="ExternalOutput").ap()

    A = nc.alloc_sbuf_tensor
    wq_sb = A("wq_sb", [128, 8 * D], BF16)
    wu_sb = A("wu_sb", [128, 8 * D], BF16)
    wo_sb = A("wo_sb", [128, 8 * D], BF16)
    x_sb = A("x_sb", [128, NCK * T], F32)
    xb_sh = A("xb_sh", [128, NCK * T], BF16)
    ln_sh = A("ln_sh", [128, NCK * T], BF16)
    q_sh = A("q_sh", [128, NCK * T], BF16)
    x1_sb = A("x1_sb", [128, NCK * T], F32)
    t1_sb = A("t1_sb", [128, T], F32)
    sq_sb = A("sq_sb", [128, 2 * T], BF16)
    kvbd = A("kvbd", [128, 1024], BF16)
    cstf = A("cstf", [128, 83], F32)
    cstb = A("cstb", [128, 401], BF16)
    bqs = cstf[:, 0:8]
    bus = cstf[:, 8:16]
    bos = cstf[:, 16:24]
    b2s = cstf[:, 24:32]
    lnws = cstf[:, 32:40]
    lnbs = cstf[:, 40:48]
    b1s = cstf[:, 48:80]
    scale_s = cstf[0:1, 80:81]
    eps_c = cstf[0:1, 81:82]
    neghalf_c = cstf[0:1, 82:83]
    ident = cstb[:, 0:128]
    oh16 = cstb[0:1, 128:384]
    oh4 = cstb[0:1, 384:400]
    mean_r = A("mean_r", [1, T], F32)
    var_r = A("var_r", [1, T], F32)
    bm_r = A("bm_r", [1, T], F32)
    ones_c = A("ones_c", [128, 1], BF16)
    ones_r = A("ones_r", [1, 128], F32)
    ones_bsub = A("ones_bsub", [1, 128], BF16)
    psum = nc.alloc_psum_tensor("psum", [128, 4096], F32)

    def bank(b, p0=0, pn=128, c0=0, cn=512):
        return psum[p0:p0 + pn, b * 512 + c0:b * 512 + c0 + cn]

    em = Em(nc)
    for s in ("sP", "sV", "sA", "dSm", "dK", "dC", "dX", "dW1a", "dW1b",
              "dW2a", "dW2b", "dOa", "dOb", "dDbg", "dLs", "dLn",
              "dF1", "dF2s", "dF2", "dXg", "dLg"):
        em.sem(s)

    bufs = {n: Buf() for n in
            ("x", "xb", "ln", "q", "x1", "h", "t1", "sq", "ktm", "vtm",
             "w1b0", "w1b1", "w2b0", "w2b1", "outc0", "outc1",
             "t_sb", "u2_sb", "rows", "stg0", "stg1", "oht")}
    pbank = [Buf() for _ in range(8)]
    tks = {}

    def lnv(c):
        return ln_sh[:, c * T:(c + 1) * T]

    def xf(c):
        return x_sb[:, c * T:(c + 1) * T]

    def x1f(c):
        return x1_sb[:, c * T:(c + 1) * T]

    def xbv(c):
        return xb_sh[:, c * T:(c + 1) * T]

    def ln_stats(src_tok_fn):
        mm1 = [None, None]
        sqw = bufs["sq"].write_waits()
        for c in range(NCK):
            tsrc = src_tok_fn(c)
            par = c % 2
            sqv = sq_sb[:, par * T:(par + 1) * T]
            wv = [tsrc, mm1[par]] + (sqw if c == 0 else [])
            ta = em.op("scalar", lambda e, c=c, sqv=sqv: e.activation(
                sqv, xbv(c), AF.Square), wv, "sA")
            w0 = ([tsrc] + ([tks["oc"]] + pbank[0].write_waits() if c == 0 else []))
            em.op("tensor", lambda e, c=c: e.matmul(
                bank(0, 0, 1), ones_c[:], xbv(c),
                start=(c == 0), stop=(c == NCK - 1), skip_group_check=True), w0)
            w1 = [ta] + (pbank[1].write_waits() if c == 0 else [])
            tm = em.op("tensor", lambda e, c=c, sqv=sqv: e.matmul(
                bank(1, 0, 1), ones_c[:], sqv,
                start=(c == 0), stop=(c == NCK - 1), skip_group_check=True),
                w1, "sP")
            mm1[par] = tm
        tp = tm
        pbank[0].wrote(tp)
        pbank[1].wrote(tp)
        bufs["sq"].wrote(tp)
        bufs["sq"].read(tp)
        return tp

    def ln_rows_bcast(pe_stats, defer_rstd=False):
        rw = bufs["rows"].write_waits()
        tmean = em.op("vector", lambda e: e.tensor_scalar_mul(
            mean_r[:], bank(0, 0, 1), 1.0 / D), [pe_stats] + rw)
        w3 = [tmean, tks["or"]] + pbank[3].write_waits()
        tb3 = em.op("tensor", lambda e: e.matmul(bank(3), ones_r[:], mean_r[:],
                                                 start=True, stop=True), w3, "sP")
        pbank[3].wrote(tb3)
        trd = em.op("vector", lambda e: e.tensor_scalar_mul(var_r[:], bank(1, 0, 1), 1.0 / D),
                    [pe_stats], "sV")
        pbank[0].read(trd)
        pbank[1].read(trd)
        em.op("vector", lambda e: e.tensor_mul(bm_r[:], mean_r[:], mean_r[:]), [])
        tvar = em.op("vector", lambda e: e.tensor_sub(var_r[:], var_r[:], bm_r[:]),
                     [], "sV")
        em.op("scalar", lambda e: e.activation(var_r[:], var_r[:], AF.Ln,
                                               bias=eps_c), [tvar], "sA")
        trow = em.op("scalar", lambda e: e.activation(var_r[:], var_r[:], AF.Exp,
                                                      scale=neghalf_c), [], "sA")
        bufs["rows"].wrote(trow)

        def emit_b2():
            w2 = [trow, tks["or"]] + pbank[2].write_waits()
            tb2 = em.op("tensor", lambda e: e.matmul(bank(2), ones_r[:], var_r[:],
                                                     start=True, stop=True),
                        w2, "sP")
            pbank[2].wrote(tb2)
            bufs["rows"].read(tb2)
            return tb2

        if defer_rstd:
            return tb3, emit_b2
        return tb3, emit_b2()

    def ln_apply(src_f32, dst_bf, tb, first_waits):
        tb3, tb2 = tb
        toks = []
        for c in range(NCK):
            wv = [tb3] + (bufs["t1"].write_waits() if c == 0 else [])
            em.op("vector", lambda e, c=c: e.tensor_sub(t1_sb[:], src_f32(c), bank(3)), wv)
            wv2 = [tb2] + (list(first_waits) if c == 0 else [])
            tv = em.op("vector", lambda e, c=c: e.tensor_mul(dst_bf(c), t1_sb[:], bank(2)),
                       wv2, "sV")
            toks.append(tv)
        bufs["t1"].wrote(toks[-1])
        pbank[2].read(toks[-1])
        pbank[3].read(toks[-1])
        return toks

    def ln_full(src_f32, src_toks, dst_bf, dst_first_waits):
        xb_toks = []

        def prod(c):
            wv = [src_toks[c]] + (bufs["xb"].write_waits() if c == 0 else [])
            tv = em.op("scalar", lambda e, c=c: e.copy(xbv(c), src_f32(c)),
                       wv, "sA")
            xb_toks.append(tv)
            return tv

        tp = ln_stats(prod)
        bufs["xb"].wrote(xb_toks[-1])
        tb = ln_rows_bcast(tp)
        return ln_apply(src_f32, dst_bf, tb, dst_first_waits)

    with (
        nc.sbuf_tensor("wk_sb", [128, 8 * D], BF16) as wk_sb,
        nc.sbuf_tensor("wv_sb", [128, 8 * D], BF16) as wv_sb,
        nc.sbuf_tensor("k_tm", [128, D], BF16) as k_tm,
        nc.sbuf_tensor("v_tm", [128, D], BF16) as v_tm,
        nc.sbuf_tensor("kv_f", [128, 512], F32) as kv_f,
        nc.sbuf_tensor("kv_acc", [128, 1024], F32) as kv_acc,
        nc.sbuf_tensor("sign_f", [128, 512], F32) as sign_f,
        nc.sbuf_tensor("bkr_bf", [1, D], BF16) as bkr_bf,
        nc.sbuf_tensor("bvr_bf", [1, D], BF16) as bvr_bf,
        nc.sbuf_tensor("cstr", [1, 2048], F32) as cstr,
    ):
        def cast_proj(dst_sb, src):
            return em.dma("gpsimd", lambda e, dst_sb=dst_sb, src=src: e.dma_start(
                out=dst_sb.ap().rearrange("p (c d) -> p c d", d=D),
                in_=src.rearrange("(c p) d -> p c d", p=128)), [], "dC")

        if 'A' in parts:
            tk_wk = cast_proj(wk_sb, wk_d)
            tk_wv = cast_proj(wv_sb, wv_d)
            tk_wq = cast_proj(wq_sb, wq_d)
            tk_wu = cast_proj(wu_sb, wu_d)
            tk_wo = cast_proj(wo_sb, wo_d)
            tk_w1 = em.dma("gpsimd", lambda e: e.dma_start(out=w1c, in_=w1tb), [], "dC")
            tk_w2 = em.dma("gpsimd", lambda e: e.dma_start(out=w2c, in_=w2tb), [], "dC")
        else:
            tk_wk = tk_wv = tk_wq = tk_wu = tk_wo = tk_w1 = tk_w2 = None

        em.dma("sync", lambda e: e.dma_start(out=cstf[:], in_=cstf_d), [], "dSm")
        tk_cst = em.dma("sync", lambda e: e.dma_start(out=cstb[:], in_=cstb_d),
                        [], "dSm")
        tk_cr = em.dma("sync", lambda e: e.dma_start(out=cstr[:], in_=cstr_d),
                       [], "dK")
        tks["cst"] = tk_cst
        tks["oc"] = em.op("vector", lambda e: e.memset(ones_c[:], 1.0), [], "sV")
        tks["or"] = em.op("vector", lambda e: e.memset(ones_r[:], 1.0), [], "sV")
        tks["ob"] = em.op("vector", lambda e: e.memset(ones_bsub[:], 1.0), [], "sV")
        em.op("scalar", lambda e: e.copy(bkr_bf[0:1, :], cstr[0:1, 0:1024]),
              [tk_cr, tk_cst], "sA")
        tk_bias = em.op("scalar", lambda e: e.copy(bvr_bf[0:1, :],
                                                   cstr[0:1, 1024:2048]), [], "sA")
        tkva = em.op("vector", lambda e: e.memset(kv_acc[:], 0.0), [tk_cst], "sV")
        em.op("vector", lambda e: e.memset(kvbd[:], 0.0), [], "sV")


        for t in range(p1_tiles):
            xw = bufs["x"].write_waits()
            tx = em.dma("sync", lambda e, t=t: e.dma_start(
                out=x_sb.ap().rearrange("p (c s) -> p c s", s=T),
                in_=xT.rearrange("(c p) (t s) -> p t c s", p=128, s=T)[:, t, :, :]),
                xw, "dX")
            bufs["x"].wrote(tx)

            ap_toks = ln_full(xf, [tx] * NCK, lnv, bufs["ln"].write_waits())
            for c in range(NCK):
                bufs["x"].read(ap_toks[c])
            bufs["ln"].wrote(ap_toks[-1])
            tst = em.dma("sync", lambda e, t=t: e.dma_start(
                out=lnx[t], in_=ln_sh[:, :]), [ap_toks[-1]], "dLs")
            bufs["ln"].read(tst)

            for u in range(NSUB):
                evs = {}
                for (w_sb, brow, dst, dst_name, do_relu, wtok) in (
                        (wk_sb, bkr_bf, k_tm, "ktm", True, tk_wk),
                        (wv_sb, bvr_bf, v_tm, "vtm", False, tk_wv)):
                    for half in range(2):
                        b = 4 + half
                        wv0 = pbank[b].write_waits() + [wtok]
                        if u == 0:
                            wv0.append(ap_toks[0])
                        for k in range(NCK):
                            wk_ = (wv0 if k == 0 else
                                   ([ap_toks[k]] if u == 0 else []))
                            em.op("tensor",
                                  lambda e, u=u, k=k, half=half, b=b, w_sb=w_sb: e.matmul(
                                      bank(b),
                                      ln_sh[:, k * T + u * 128:k * T + (u + 1) * 128],
                                      w_sb[:, k * D + half * 512:k * D + (half + 1) * 512],
                                      start=(k == 0), stop=False,
                                      skip_group_check=True),
                                  wk_)
                        em.op("tensor", lambda e, b=b, half=half, brow=brow: e.matmul(
                            bank(b), ones_bsub[:],
                            brow[0:1, half * 512:(half + 1) * 512],
                            start=False, stop=True, skip_group_check=True),
                            [tk_bias, tks["ob"]])
                        tp = em.retro_inc("tensor", "sP")
                        pbank[b].wrote(tp)
                        wv1 = [tp] + bufs[dst_name].write_waits()
                        if do_relu:
                            tev = em.op("vector",
                                        lambda e, half=half, b=b, dst=dst: e.tensor_scalar_max(
                                            dst[:, half * 512:(half + 1) * 512],
                                            bank(b), 0.0), wv1, "sV")
                        else:
                            tev = em.op("scalar",
                                        lambda e, half=half, b=b, dst=dst: e.copy(
                                            dst[:, half * 512:(half + 1) * 512],
                                            bank(b)), wv1, "sA")
                        pbank[b].read(tev)
                        bufs[dst_name].wrote(tev)
                        evs[(dst_name, half)] = tev
                for b, h in ((6, 0), (7, 1)):
                    for j in range(4):
                        p = h * 4 + j
                        slot = j * 128
                        wv = ([evs[("ktm", h)], evs[("vtm", h)], tkva]
                              + pbank[b].write_waits()) if j == 0 else []
                        em.op("tensor",
                              lambda e, p=p, b=b, slot=slot: e.matmul(
                                  bank(b, 0, 128, slot, 128),
                                  k_tm[:, p * 128:(p + 1) * 128],
                                  v_tm[:, p * 128:(p + 1) * 128],
                                  start=True, stop=True, skip_group_check=True),
                              wv)
                    tkv = em.retro_inc("tensor", "sP")
                    bufs["ktm"].read(tkv)
                    bufs["vtm"].read(tkv)
                    pbank[b].wrote(tkv)
                    tka = em.op("vector", lambda e, b=b, h=h: e.tensor_add(
                        kv_acc[:, h * 512:(h + 1) * 512],
                        kv_acc[:, h * 512:(h + 1) * 512], bank(b)), [tkv], "sV")
                    pbank[b].read(tka)

        pe_p1_end = em.tok("sP")

        for p in (range(8) if 'D' in parts else []):
            off = (p // 4) * 512 + (p % 4) * 128
            em.op("vector", lambda e, p=p, off=off: e.tensor_copy(
                kv_f[0:64, p * 64:p * 64 + 64],
                kv_acc[0:64, off:off + 64]), [])
            em.op("vector", lambda e, p=p, off=off: e.tensor_copy(
                kv_f[64:128, p * 64:p * 64 + 64],
                kv_acc[64:128, off + 64:off + 128]), [])
        if 'D' in parts:
            tcp = em.op("vector", lambda e: e.engine_nop(), [], "sV")
            tsg = em.op("scalar", lambda e: e.activation(sign_f[:], kv_f[:], AF.Sign),
                        [tcp], "sA")
            tab = em.op("scalar", lambda e: e.activation(kv_f[:], kv_f[:], AF.Abs),
                        [], "sA")
            em.op("vector", lambda e: e.tensor_scalar_max(kv_f[:], kv_f[:], 0.01), [tab])
            em.op("vector", lambda e: e.tensor_scalar_min(kv_f[:], kv_f[:], 100.0), [])
            em.op("vector", lambda e: e.tensor_mul(kv_f[:], kv_f[:], sign_f[:]), [])
            tclamp = None
            for p in range(8):
                em.op("vector", lambda e, p=p: e.tensor_copy(
                    kvbd[0:64, p * 128:p * 128 + 64],
                    kv_f[0:64, p * 64:p * 64 + 64]), [])
                tclamp = em.op("vector", lambda e, p=p: e.tensor_copy(
                    kvbd[64:128, p * 128 + 64:p * 128 + 128],
                    kv_f[64:128, p * 64:p * 64 + 64]), [], "sV")
        else:
            tclamp = None

        if dbg:
            em.dma("gpsimd", lambda e: e.dma_start(out=dbg_out["dbg_kv"], in_=kv_f[:]),
                   [tclamp], "dDbg")

        scopeA_done = [tclamp, em.tok("sP"), em.tok("sV"), em.tok("sA"),
                       em.tok("dDbg")]
        em.flush("blk1")

    import contextlib
    with contextlib.ExitStack() as _st:
        def SB(name, shape, dt):
            return _st.enter_context(nc.sbuf_tensor(name, shape, dt))
        h_sb = SB("h_sb", [128, NHC * T], BF16)
        w1blk = SB("w1blk", [128, 2 * 4096], BF16)
        w2blk = SB("w2blk", [128, 2 * 4096], BF16)
        that4 = SB("that4", [4, 4096], BF16)
        ehat = SB("ehat", [16, 1024], BF16)
        ohat = SB("ohat", [128, 512], BF16)
        odd1g = SB("odd1g", [128, 32], BF16)
        row128 = SB("row128", [1, 128], BF16)
        t_sb = SB("t_sb", [4, T], BF16)
        u2_sb = SB("u2_sb", [16, T], BF16)
        outc = SB("outc", [128, 2 * T], F32)
        o4 = SB("o4", [4, 1024], BF16)
        ohT = SB("ohT", [16, 512], BF16)
        stage2 = SB("stage2", [128, 2048], BF16)
        e2rows = [h_sb[0:1, m * 1024:(m + 1) * 1024] for m in range(4)]

        p2x = {}
        p2ln = {}
        fw1_tok = {}
        fw2_tok = {}

        def emit_ln1_load(t, eng, extra):
            tk = em.dma(eng, lambda e, t=t: e.dma_start(
                out=ln_sh[:, :], in_=lnx[t]),
                bufs["ln"].write_waits() + extra,
                "dLn" if eng == "sync" else "dLg")
            bufs["ln"].wrote(tk)
            p2ln[t] = tk

        def emit_x_load(t, eng):
            tk = em.dma(eng, lambda e, t=t: e.dma_start(
                out=x_sb.ap().rearrange("p (c s) -> p c s", s=T),
                in_=xT.rearrange("(c p) (t s) -> p t c s", p=128, s=T)[:, t, :, :]),
                bufs["x"].write_waits(), "dX" if eng == "sync" else "dXg")
            bufs["x"].wrote(tk)
            p2x[t] = tk

        for t in range(p2_tiles):
            if t == 0:
                emit_ln1_load(0, "sync", [em.tok("dLs"), pe_p1_end])
                emit_x_load(0, "sync")
            tln1 = p2ln[t]
            tx = p2x[t]

            q_toks = []
            for j in range(NCK):
                b = 4 + (j % 2)
                wv = pbank[b].write_waits() + [tk_wq, tln1]
                for k in range(NCK):
                    em.op("tensor", lambda e, j=j, k=k, b=b: e.matmul(
                        bank(b), wq_sb[:, k * D + j * 128:k * D + (j + 1) * 128],
                        lnv(k), start=(k == 0), stop=(k == NCK - 1),
                        skip_group_check=True),
                        wv if k == 0 else [])
                tp = em.retro_inc("tensor", "sP")
                pbank[b].wrote(tp)
                tq = em.op("scalar", lambda e, j=j, b=b: e.activation(
                    q_sh[:, j * T:(j + 1) * T], bank(b), AF.Relu,
                    bias=bqs[:, j:j + 1]), [tp], "sA")
                pbank[b].read(tq)
                q_toks.append(tq)
            bufs["q"].wrote(q_toks[-1])
            if dbg and t == 0:
                bufs["q"].read(em.dma("gpsimd", lambda e: e.dma_start(
                    out=dbg_out["dbg_q"].rearrange("p (c s) -> p c s", s=128),
                    in_=q_sh.ap().rearrange("p (c s) -> p c s", s=T)[:, :, 0:128]),
                    [q_toks[-1]], "dDbg"))

            attn_toks = []
            xbw = bufs["xb"].write_waits()
            for j in range(NCK):
                b = 6 + (j % 2)
                wv = pbank[b].write_waits() + [tclamp, q_toks[j]]
                em.op("tensor", lambda e, j=j, b=b: e.matmul(
                    bank(b), kvbd[:, j * 128:(j + 1) * 128],
                    q_sh[:, j * T:(j + 1) * T],
                    start=True, stop=True, skip_group_check=True), wv)
                tp = em.retro_inc("tensor", "sP")
                pbank[b].wrote(tp)
                tev = em.op("scalar", lambda e, j=j, b=b: e.copy(
                    xbv(j), bank(b)), [tp] + (xbw if j == 0 else []), "sA")
                pbank[b].read(tev)
                attn_toks.append(tev)
            bufs["xb"].wrote(attn_toks[-1])
            if dbg and t == 0:
                bufs["xb"].read(em.dma("gpsimd", lambda e: e.dma_start(
                    out=dbg_out["dbg_attn"].rearrange("p (c s) -> p c s", s=128),
                    in_=xb_sh.ap().rearrange("p (c s) -> p c s", s=T)[:, :, 0:128]),
                    [attn_toks[-1]], "dDbg"))

            if p2_level < 2:
                continue
            tp2 = ln_stats(lambda c: attn_toks[c])
            tb2m, emit_b2 = ln_rows_bcast(tp2, defer_rstd=True)
            tb2r = None

            prod_toks = []
            for c in range(NCK):
                b = 4 + (c % 4)
                wv = pbank[b].write_waits() + [tk_wu]
                if c == 0:
                    wv.append(tln1)
                for k in range(NCK):
                    em.op("tensor", lambda e, c=c, k=k, b=b: e.matmul(
                        bank(b), wu_sb[:, k * D + c * 128:k * D + (c + 1) * 128],
                        lnv(k), start=(k == 0), stop=(k == NCK - 1),
                        skip_group_check=True), wv if k == 0 else [])
                tp = em.retro_inc("tensor", "sP")
                if c == 0:
                    tb2r = emit_b2()
                pbank[b].wrote(tp)
                em.op("vector", lambda e, c=c: e.tensor_sub(
                    t1_sb[:], xbv(c), bank(3)),
                    [tb2m] + bufs["t1"].write_waits())
                tv2_ = em.op("vector", lambda e: e.tensor_mul(t1_sb[:], t1_sb[:], bank(2)),
                             [tb2r], "sV")
                bufs["t1"].wrote(tv2_)
                ts2 = em.op("scalar", lambda e, c=c: e.activation(
                    t1_sb[:], t1_sb[:], AF.Identity,
                    bias=lnbs[:, c:c + 1], scale=lnws[:, c:c + 1]), [tv2_], "sA")
                tpr = em.op("vector", lambda e, c=c, b=b: e.scalar_tensor_tensor(
                    q_sh[:, c * T:(c + 1) * T], bank(b), bus[:, c:c + 1], t1_sb[:],
                    ALU.add, ALU.mult),
                    [tp, ts2] + (bufs["q"].write_waits() if c == 0 else []), "sV")
                bufs["t1"].wrote(tpr)
                pbank[b].read(tpr)
                pbank[2].read(tpr)
                pbank[3].read(tpr)
                prod_toks.append(tpr)
            bufs["q"].wrote(prod_toks[-1])

            x1w = bufs["x1"].write_waits()
            x1_toks = []
            for j in range(NCK):
                b = 6 + (j % 2)
                wv = pbank[b].write_waits() + [tk_wo]
                for k in range(NCK):
                    em.op("tensor", lambda e, j=j, k=k, b=b: e.matmul(
                        bank(b), wo_sb[:, k * D + j * 128:k * D + (j + 1) * 128],
                        q_sh[:, k * T:(k + 1) * T], start=(k == 0),
                        stop=(k == NCK - 1), skip_group_check=True),
                        (wv + [prod_toks[k]]) if k == 0 or j == 0 else [])
                tp = em.retro_inc("tensor", "sP")
                pbank[b].wrote(tp)
                tev = em.op("vector", lambda e, j=j, b=b: e.scalar_tensor_tensor(
                    x1f(j), bank(b), bos[:, j:j + 1], xf(j), ALU.add, ALU.add),
                    [tp, tx] + (x1w if j == 0 else []), "sV")
                pbank[b].read(tev)
                bufs["x"].read(tev)
                x1_toks.append(tev)
            bufs["x1"].wrote(x1_toks[-1])
            if t + 1 < p2_tiles:
                emit_x_load(t + 1, "gpsimd")
            if dbg and t == 0:
                bufs["x1"].read(em.dma("gpsimd", lambda e: e.dma_start(
                    out=dbg_out["dbg_x1"].rearrange("p (c s) -> p c s", s=128),
                    in_=x1_sb.ap().rearrange("p (c s) -> p c s", s=T)[:, :, 0:128]),
                    [x1_toks[-1]], "dDbg"))

            if p2_level < 3:
                continue
            ln_war = em.tok("sP")
            ap3 = ln_full(x1f, x1_toks, lnv,
                          bufs["ln"].write_waits() + [ln_war])
            for c in range(NCK):
                bufs["x1"].read(ap3[c])
            bufs["ln"].wrote(ap3[-1])
            if dbg and t == 0:
                bufs["ln"].read(em.dma("gpsimd", lambda e: e.dma_start(
                    out=dbg_out["dbg_ln3"].rearrange("p (c s) -> p c s", s=128),
                    in_=ln_sh.ap().rearrange("p (c s) -> p c s", s=T)[:, :, 0:128]),
                    [ap3[-1]], "dDbg"))

            if t == 0:
                for k in range(NCK):
                    em.op("vector", lambda e, k=k: e.tensor_copy(
                        odd1g[:, k * 4:(k + 1) * 4],
                        ln_sh.ap()[:, k * T + 2:k * T + 9:2]),
                        [ap3[-1]] if k == 0 else [])
                em.op("vector", lambda e: e.memset(ohat[:], 0.0), [])
                for jc in range(NHC):
                    r, k = jc // 8, jc % 8
                    em.op("vector", lambda e, jc=jc, r=r, k=k: e.tensor_copy(
                        ohat[:, jc * 16 + 4 * r:jc * 16 + 4 * r + 4],
                        ln_sh.ap()[:, k * T + 10:k * T + 17:2]), [], "sV")
                for c in range(NHC):
                    i, k = c // 8, c % 8
                    wv = pbank[4].write_waits()
                    tp1 = em.op("tensor", lambda e, i=i, k=k: e.transpose(
                        bank(4, 0, 1, 0, 64).bitcast(BF16),
                        ln_sh[:, k * T + 1 + 2 * i:k * T + 2 + 2 * i], ident[:]),
                        wv + [ap3[-1], tks["cst"]], "sP")
                    pbank[4].wrote(tp1)
                    tv1 = em.op("vector", lambda e: e.tensor_scalar_mul(
                        row128[:], bank(4, 0, 1, 0, 64).bitcast(BF16),
                        scale_s[0:1, 0:1]), [tp1], "sV")
                    pbank[4].read(tv1)
                    wv = pbank[5].write_waits()
                    tp2_ = em.op("tensor", lambda e, i=i: e.matmul(
                        bank(5, 0, 4, 0, 128), oh4[0:1, i * 4:(i + 1) * 4],
                        row128[:], start=True, stop=True, skip_group_check=True),
                        wv + [tv1], "sP")
                    pbank[5].wrote(tp2_)
                    tv2_ = em.op("vector", lambda e, c=c: e.tensor_copy(
                        that4[0:4, c * 128:(c + 1) * 128], bank(5, 0, 4, 0, 128)),
                        [tp2_], "sV")
                    pbank[5].read(tv2_)
                for m in range(4):
                    for k in range(NCK):
                        wv = pbank[4].write_waits()
                        tp1 = em.op("tensor", lambda e, m=m, k=k: e.transpose(
                            bank(4, 0, 1, 0, 64).bitcast(BF16),
                            ln_sh[:, k * T + 9 + 2 * m:k * T + 10 + 2 * m],
                            ident[:]), wv, "sP")
                        pbank[4].wrote(tp1)
                        tv1 = em.op("vector", lambda e, m=m, k=k: e.tensor_copy(
                            e2rows[m][0:1, k * 128:(k + 1) * 128],
                            bank(4, 0, 1, 0, 64).bitcast(BF16)), [tp1], "sV")
                        pbank[4].read(tv1)
                te2 = None
                for m in range(4):
                    te2 = em.op("vector", lambda e, m=m: e.tensor_scalar_mul(
                        e2rows[m][:], e2rows[m][:], scale_s[0:1, 0:1]), [], "sV")
                for c in range(NCK):
                    m = c // 2
                    wv = pbank[5].write_waits()
                    for r in range(4):
                        rho = 4 * r + m
                        st = 512 * (c % 2) + r
                        em.op("tensor",
                              lambda e, m=m, rho=rho, st=st, r=r: e.matmul(
                                  bank(5, 0, 16, 0, 128),
                                  oh16[0:1, rho * 16:(rho + 1) * 16],
                                  e2rows[m][0:1, st:st + 509:4],
                                  start=(r == 0), stop=(r == 3),
                                  skip_group_check=True),
                              (wv + [te2]) if r == 0 else [])
                    tp = em.retro_inc("tensor", "sP")
                    pbank[5].wrote(tp)
                    tv = em.op("vector", lambda e, c=c: e.tensor_copy(
                        ehat[0:16, c * 128:(c + 1) * 128], bank(5, 0, 16, 0, 128)),
                        [tp], "sV")
                    pbank[5].read(tv)
                bufs["t_sb"].wrote(tv)
                bufs["h"].read(tp)
                t_o1g = em.tok("sV")
                for k in range(NCK):
                    wv = pbank[4].write_waits()
                    tp1 = em.op("tensor", lambda e, k=k: e.transpose(
                        bank(4, 0, 4, 0, 64).bitcast(BF16),
                        odd1g[:, k * 4:(k + 1) * 4], ident[:]),
                        wv + [t_o1g], "sP")
                    pbank[4].wrote(tp1)
                    tv1 = em.op("vector", lambda e, k=k: e.tensor_copy(
                        o4[0:4, k * 128:(k + 1) * 128],
                        bank(4, 0, 4, 0, 64).bitcast(BF16)), [tp1], "sV")
                    pbank[4].read(tv1)
                t_mach = em.tok("sV")

            if t == 0:
                wv = pbank[0].write_waits() + [bufs["t_sb"].writer, ap3[0]]
                for k in range(NCK):
                    em.op("tensor", lambda e, k=k: e.matmul(
                        bank(0, 0, 4), odd1g[:, k * 4:(k + 1) * 4], lnv(k),
                        start=(k == 0), stop=(k == NCK - 1),
                        skip_group_check=True),
                        wv if k == 0 else [ap3[k]])
                tpt = em.retro_inc("tensor", "sP")
                pbank[0].wrote(tpt)
                ttv = em.op("vector", lambda e: e.tensor_copy(t_sb[:], bank(0, 0, 4)),
                            [tpt] + bufs["t_sb"].write_waits(), "sV")
                pbank[0].read(ttv)
                bufs["t_sb"].wrote(ttv)
                if dbg:
                    bufs["t_sb"].read(em.dma("gpsimd", lambda e: e.dma_start(
                        out=dbg_out["dbg_t"], in_=t_sb[:]), [ttv], "dDbg"))

            if p2_level < 4:
                continue
            hw = bufs["h"].write_waits()
            u2w = pbank[1].write_waits() if t == 0 else []
            h_toks = []
            w1_toks = [None] * 8

            def load_w1(jcb):
                wbuf = jcb % 2
                bname = "w1b%d" % wbuf
                src = w1c if t == 0 else w1f
                extra = [tk_w1] if t == 0 else [fw1_tok[jcb]]
                tw = em.dma("sync", lambda e, jcb=jcb, wbuf=wbuf, src=src: e.dma_start(
                    out=w1blk[:, wbuf * 4096:(wbuf + 1) * 4096], in_=src[jcb]),
                    bufs[bname].write_waits() + extra
                    + (scopeA_done if t == 0 and jcb < 2 else []),
                    "dW1a" if wbuf == 0 else "dW1b")
                w1_toks[jcb] = tw
                bufs[bname].wrote(tw)

            def fold_w1(jcb):
                wbuf = jcb % 2
                bname = "w1b%d" % wbuf
                rd = bufs[bname].write_waits()
                ta = None
                for k in range(NCK):
                    bf = 2 + (k % 2)
                    wvf = pbank[bf].write_waits() + ([t_mach] if k < 2 else [])
                    tm = em.op("tensor", lambda e, k=k, bf=bf, jcb=jcb: e.matmul(
                        bank(bf), o4[0:4, k * 128:(k + 1) * 128],
                        that4[0:4, jcb * 512:(jcb + 1) * 512],
                        start=True, stop=True, skip_group_check=True), wvf, "sP")
                    pbank[bf].wrote(tm)
                    ta = em.op("vector", lambda e, k=k, bf=bf, wbuf=wbuf: e.tensor_add(
                        w1blk[:, wbuf * 4096 + k * 512:wbuf * 4096 + (k + 1) * 512],
                        w1blk[:, wbuf * 4096 + k * 512:wbuf * 4096 + (k + 1) * 512],
                        bank(bf)), [tm] + (rd if k == 0 else []), "sV")
                    pbank[bf].read(ta)
                ts = em.dma("sync", lambda e, jcb=jcb, wbuf=wbuf: e.dma_start(
                    out=w1f[jcb], in_=w1blk[:, wbuf * 4096:(wbuf + 1) * 4096]),
                    [ta], "dF1")
                bufs[bname].read(ts)
                fw1_tok[jcb] = ts

            def fold_w2(jb):
                ohw = bufs["oht"].write_waits()
                cps = []
                for jsub in range(4):
                    jc = jb * 4 + jsub
                    bf = 6 + (jsub % 2)
                    wvf = pbank[bf].write_waits() + ([t_mach] if jsub < 2 else [])
                    tp1 = em.op("tensor", lambda e, jc=jc, bf=bf: e.transpose(
                        bank(bf, 0, 16, 0, 64).bitcast(BF16),
                        ohat[:, jc * 16:(jc + 1) * 16], ident[:]), wvf, "sP")
                    pbank[bf].wrote(tp1)
                    tcp = em.op("vector", lambda e, jsub=jsub, bf=bf: e.tensor_copy(
                        ohT[0:16, jsub * 128:(jsub + 1) * 128],
                        bank(bf, 0, 16, 0, 64).bitcast(BF16)),
                        [tp1] + (ohw if jsub == 0 else []), "sV")
                    pbank[bf].read(tcp)
                    cps.append(tcp)
                tls = {}

                def stage_load(jsub):
                    hb = jsub % 2
                    tls[jsub] = em.dma("gpsimd", lambda e, jb=jb, jsub=jsub, hb=hb:
                                       e.dma_start(
                                           out=stage2[:, hb * 1024:(hb + 1) * 1024],
                                           in_=w2c[jb][:, jsub * 1024:(jsub + 1) * 1024]),
                                       bufs["stg%d" % hb].write_waits() + [tk_w2],
                                       "dF2s")

                stage_load(0)
                stage_load(1)
                tm = None
                for jsub in range(4):
                    hb = jsub % 2
                    ta = None
                    for hh in range(2):
                        bf = 0 + hh
                        wvf = pbank[bf].write_waits() + [cps[jsub]]
                        tm = em.op("tensor",
                                   lambda e, jsub=jsub, bf=bf, hh=hh: e.matmul(
                                       bank(bf),
                                       ohT[0:16, jsub * 128:(jsub + 1) * 128],
                                       ehat[0:16, hh * 512:(hh + 1) * 512],
                                       start=True, stop=True,
                                       skip_group_check=True), wvf, "sP")
                        pbank[bf].wrote(tm)
                        ta = em.op("vector",
                                   lambda e, hb=hb, hh=hh, bf=bf: e.tensor_add(
                                       stage2[:, hb * 1024 + hh * 512:
                                              hb * 1024 + (hh + 1) * 512],
                                       stage2[:, hb * 1024 + hh * 512:
                                              hb * 1024 + (hh + 1) * 512],
                                       bank(bf)), [tm, tls[jsub]], "sV")
                        pbank[bf].read(ta)
                    ts = em.dma("sync", lambda e, jb=jb, jsub=jsub, hb=hb:
                                e.dma_start(
                                    out=w2f[jb][:, jsub * 1024:(jsub + 1) * 1024],
                                    in_=stage2[:, hb * 1024:(hb + 1) * 1024]),
                                [ta], "dF2")
                    bufs["stg%d" % hb].read(ts)
                    if jsub + 2 < 4:
                        stage_load(jsub + 2)
                bufs["oht"].read(tm)
                fw2_tok[jb] = ts

            load_w1(0)
            load_w1(1)
            for jcb in range(8):
                wbuf = jcb % 2
                bname = "w1b%d" % wbuf
                for js in range(4):
                    jc = jcb * 4 + js
                    b = 4 + (jc % 2)
                    wv = pbank[b].write_waits() + [w1_toks[jcb], ap3[0]]
                    if t == 0:
                        wv.append(bufs["t_sb"].writer)
                    for k in range(NCK):
                        em.op("tensor",
                              lambda e, js=js, k=k, b=b, wbuf=wbuf, t=t: e.matmul(
                                  bank(b),
                                  w1blk[:, wbuf * 4096 + k * 512 + js * 128:
                                        wbuf * 4096 + k * 512 + (js + 1) * 128],
                                  lnv(k), start=(k == 0),
                                  stop=(t > 0 and k == NCK - 1),
                                  skip_group_check=True),
                              wv if k == 0 else [ap3[k]])
                    if t == 0:
                        em.op("tensor", lambda e, jc=jc, b=b: e.matmul(
                            bank(b), that4[0:4, jc * 128:(jc + 1) * 128], t_sb[:],
                            start=False, stop=True, skip_group_check=True), [])
                    tp = em.retro_inc("tensor", "sP")
                    pbank[b].wrote(tp)
                    bufs[bname].read(tp)
                    if t == 0:
                        bufs["t_sb"].read(tp)
                    th = em.op("scalar", lambda e, jc=jc, b=b: e.activation(
                        h_sb[:, jc * T:(jc + 1) * T], bank(b), AF.Gelu,
                        bias=b1s[:, jc:jc + 1]),
                        [tp] + (hw if jc == 0 else []), "sA")
                    pbank[b].read(th)
                    h_toks.append(th)
                if t == 0 and p2_level >= 5:
                    fold_w1(jcb)
                    fold_w2(jcb)
                if jcb + 2 < 8:
                    load_w1(jcb + 2)
            bufs["h"].wrote(h_toks[-1])
            bufs["ln"].read(em.tok("sP"))
            if t + 1 < p2_tiles:
                emit_ln1_load(t + 1, "gpsimd", [])
            if t == 0:
                u2w = u2w + pbank[1].write_waits()
                for jc in range(NHC):
                    em.op("tensor", lambda e, jc=jc: e.matmul(
                        bank(1, 0, 16), ohat[:, jc * 16:(jc + 1) * 16],
                        h_sb[:, jc * T:(jc + 1) * T],
                        start=(jc == 0), stop=(jc == NHC - 1),
                        skip_group_check=True),
                        ([h_toks[-1], bufs["t_sb"].writer] + u2w) if jc == 0 else [])
                tpu2 = em.retro_inc("tensor", "sP")
                pbank[1].wrote(tpu2)
                tu2 = em.op("vector", lambda e: e.tensor_copy(u2_sb[:], bank(1, 0, 16)),
                            [tpu2] + bufs["u2_sb"].write_waits(), "sV")
                pbank[1].read(tu2)
                bufs["u2_sb"].wrote(tu2)
            if dbg and t == 0:
                bufs["u2_sb"].read(em.dma("gpsimd", lambda e: e.dma_start(
                    out=dbg_out["dbg_u2"], in_=u2_sb[:]), [tu2], "dDbg"))
                bufs["h"].read(em.dma("gpsimd", lambda e: e.dma_start(
                    out=dbg_out["dbg_h"], in_=h_sb[:, 0:512]), [h_toks[-1]], "dDbg"))

            if p2_level < 5:
                continue
            w2_toks = [None] * 8

            def load_w2(jb):
                wbuf = jb % 2
                bname = "w2b%d" % wbuf
                src = w2c if t == 0 else w2f
                extra = [tk_w2] if t == 0 else [fw2_tok[jb]]
                tw = em.dma("sync", lambda e, jb=jb, wbuf=wbuf, src=src: e.dma_start(
                    out=w2blk[:, wbuf * 4096:(wbuf + 1) * 4096], in_=src[jb]),
                    bufs[bname].write_waits() + extra
                    + (scopeA_done if t == 0 and jb < 2 else []),
                    "dW2a" if wbuf == 0 else "dW2b")
                w2_toks[jb] = tw
                bufs[bname].wrote(tw)

            load_w2(0)
            load_w2(1)
            bank_wv = []
            for b in range(8):
                bank_wv += pbank[b].write_waits()
            for jb in range(8):
                wbuf = jb % 2
                bname = "w2b%d" % wbuf
                for jsub in range(4):
                    jc = jb * 4 + jsub
                    for c in range(8):
                        if jc == 0 and c == 0:
                            wv = bank_wv + [w2_toks[jb], h_toks[0]]
                        elif c == 0:
                            wv = [w2_toks[jb], h_toks[jc]]
                        else:
                            wv = []
                        em.op("tensor",
                              lambda e, jsub=jsub, c=c, jc=jc, wbuf=wbuf, t=t: e.matmul(
                                  bank(c),
                                  w2blk[:, wbuf * 4096 + jsub * 1024 + c * 128:
                                        wbuf * 4096 + jsub * 1024 + (c + 1) * 128],
                                  h_sb[:, jc * T:(jc + 1) * T],
                                  start=(jc == 0),
                                  stop=(t > 0 and jc == NHC - 1),
                                  skip_group_check=True), wv)
                    tp = em.retro_inc("tensor", "sP")
                    bufs[bname].read(tp)
                    bufs["h"].read(tp)
                if jb + 2 < 8:
                    load_w2(jb + 2)
            tp_last = tp
            for c in (4, 5, 6, 7, 0, 1, 2, 3):
                if t == 0:
                    em.op("tensor", lambda e, c=c: e.matmul(
                        bank(c), ehat[0:16, c * 128:(c + 1) * 128], u2_sb[:],
                        start=False, stop=True, skip_group_check=True),
                        [tu2] if c == 4 else [])
                    tp = em.retro_inc("tensor", "sP")
                    bufs["u2_sb"].read(tp)
                else:
                    tp = tp_last
                pbank[c].wrote(tp)
                ob = c % 2
                obname = "outc%d" % ob
                tev = em.op("vector", lambda e, c=c, ob=ob: e.scalar_tensor_tensor(
                    outc[:, ob * T:(ob + 1) * T], bank(c), b2s[:, c:c + 1],
                    x1f(c), ALU.add, ALU.add),
                    [tp, ap3[-1]] + bufs[obname].write_waits(), "sV")
                pbank[c].read(tev)
                bufs["x1"].read(tev)
                tdma = em.dma("gpsimd", lambda e, c=c, t=t, ob=ob: e.dma_start(
                    out=outT.rearrange("(c p) (t s) -> p c t s", p=128, s=T)[:, c, t, :],
                    in_=outc[:, ob * T:(ob + 1) * T]), [tev],
                    "dOa" if ob == 0 else "dOb")
                bufs[obname].wrote(tev)
                bufs[obname].read(tdma)

        em.op("gpsimd", lambda e: e.engine_nop(),
              [em.tok("dOa"), em.tok("dOb"), em.tok("dDbg")])
        em.flush("blk2")

    return nc


def _host_layout(inputs):
    q = inputs
    xs = np.ascontiguousarray(np.asarray(q["L0_attn_query"], dtype=np.float32))
    W1 = np.asarray(q["L0_ffn_W1"], dtype=np.float32)
    W2 = np.asarray(q["L0_ffn_W2"], dtype=np.float32)
    w1t = np.ascontiguousarray(W1.T)
    w2t = np.ascontiguousarray(W2.T)
    w1tb = np.ascontiguousarray(
        w1t.reshape(8, 128, 8, 512).transpose(2, 1, 0, 3)).reshape(8, 128, 4096)
    w2tb = np.ascontiguousarray(
        w2t.reshape(8, 4, 128, 1024).transpose(0, 2, 1, 3)).reshape(8, 128, 4096)

    def pp(v, n):
        return np.ascontiguousarray(np.asarray(v, dtype=np.float32).reshape(n, 128).T)

    cst_f = np.zeros((128, 83), dtype=np.float32)
    cst_f[:, 0:8] = pp(q["L0_attn_bq"], 8)
    cst_f[:, 8:16] = pp(q["L0_attn_bu"], 8)
    cst_f[:, 16:24] = pp(q["L0_attn_bo"], 8)
    cst_f[:, 24:32] = pp(q["L0_ffn_b2"], 8)
    cst_f[:, 32:40] = pp(q["L0_attn_ln_w"], 8)
    cst_f[:, 40:48] = pp(q["L0_attn_ln_b"], 8)
    cst_f[:, 48:80] = pp(q["L0_ffn_b1"], 32)
    cst_f[0, 80] = np.asarray(q["L0_ffn_scale"], dtype=np.float32).reshape(())
    cst_f[0, 81] = 1e-5
    cst_f[0, 82] = -0.5

    cst_b = np.zeros((128, 401), dtype=ml_dtypes.bfloat16)
    cst_b[:, 0:128] = np.eye(128, dtype=ml_dtypes.bfloat16)
    for rho in range(16):
        cst_b[0, 128 + rho * 16 + rho] = 1
    for i in range(4):
        cst_b[0, 384 + i * 4 + i] = 1

    cst_r = np.zeros((1, 2048), dtype=np.float32)
    cst_r[0, 0:1024] = np.asarray(q["L0_attn_bk"], dtype=np.float32)
    cst_r[0, 1024:2048] = np.asarray(q["L0_attn_bv"], dtype=np.float32)

    common = {
        "wq": np.ascontiguousarray(np.asarray(q["L0_attn_Wq"], dtype=np.float32)),
        "wk": np.ascontiguousarray(np.asarray(q["L0_attn_Wk"], dtype=np.float32)),
        "wv": np.ascontiguousarray(np.asarray(q["L0_attn_Wv"], dtype=np.float32)),
        "wu": np.ascontiguousarray(np.asarray(q["L0_attn_Wu"], dtype=np.float32)),
        "wo": np.ascontiguousarray(np.asarray(q["L0_attn_Wo"], dtype=np.float32)),
        "w1tb": w1tb, "w2tb": w2tb,
        "cst_f": cst_f, "cst_b": cst_b, "cst_r": cst_r,
    }
    return [dict(common, xT=np.ascontiguousarray(xs[i].T))
            for i in range(xs.shape[0])]


def kernel(**inputs):
    in_maps = _host_layout(inputs)
    nc = build_nc(dbg=DBG)
    res = run_bass_kernel_spmd(nc, in_maps, core_ids=list(range(8)))
    outs = [np.ascontiguousarray(np.asarray(r["outT"]).T) for r in res.results]
    return np.stack(outs, axis=0)


# version 17
# speedup vs baseline: 1.0084x; 1.0084x over previous
import numpy as np
import ml_dtypes
import concourse.bass as bass
import concourse.mybir as mybir
from concourse.bass_utils import run_bass_kernel_spmd

F32 = mybir.dt.float32
BF16 = mybir.dt.bfloat16
AF = mybir.ActivationFunctionType
ALU = mybir.AluOpType

D = 1024
NCK = 8
T = 512
NT = 8
NSUB = 4
NHC = 32
NH = 16
HD = 64
EPS = 1e-5
ENGINES = ("tensor", "vector", "scalar", "sync", "gpsimd")

DBG = False


class Em:

    def __init__(self, nc):
        self.nc = nc
        self.q = {e: [] for e in ENGINES}
        self.sems = {}
        self.watermark = {}

    def sem(self, name):
        self.sems[name] = [self.nc.alloc_semaphore(name), 0]

    AUTO = {"vector": "sV", "scalar": "sA"}

    def op(self, eng, fn, waits=(), inc=None, n=1):
        ws = [w for w in waits if w is not None and w[1] > 0]
        auto = self.AUTO.get(eng)
        if auto is not None:
            prev = self.sems[auto][1]
            if prev > 0:
                ws.append((auto, prev))
            inc = auto
            n = 1
        self.q[eng].append((ws, fn, inc, n))
        if inc is not None:
            self.sems[inc][1] += n
            return (inc, self.sems[inc][1])
        return None

    def retro_inc(self, eng, sem, n=1):
        ws, fn, inc, ni = self.q[eng][-1]
        assert inc is None, "last op already increments a semaphore"
        self.q[eng][-1] = (ws, fn, sem, n)
        self.sems[sem][1] += n
        return (sem, self.sems[sem][1])

    def dma(self, eng, fn, waits=(), inc=None):
        return self.op(eng, fn, list(waits) + [self.tok(inc)], inc, 16)

    def tok(self, name):
        return (name, self.sems[name][1])

    def flush(self, name):
        qs, self.q = self.q, {e: [] for e in ENGINES}
        nc = self.nc

        def mk(eng_name):
            def body(eng):
                for ws, fn, inc, n in qs[eng_name]:
                    need = []
                    for sname, val in ws:
                        key = (eng_name, sname)
                        if self.watermark.get(key, 0) < val:
                            need.append((sname, val))
                            self.watermark[key] = val
                    for sname, val in need[:-1]:
                        eng.wait_ge(self.sems[sname][0], val)
                    inst = fn(eng)
                    if need:
                        sname, val = need[-1]
                        inst._wait_ge(self.sems[sname][0], val)
                    if inc is not None:
                        inst.then_inc(self.sems[inc][0], n)
            return body

        with nc.Block(name) as block:
            for e in ENGINES:
                if qs[e]:
                    getattr(block, e)(mk(e))


class Buf:
    def __init__(self):
        self.writer = None
        self.readers = []

    def read(self, tok):
        if tok is not None:
            self.readers.append(tok)

    def write_waits(self):
        r, self.readers = self.readers, []
        return r

    def wrote(self, tok):
        self.writer = tok


def build_nc(dbg=False, p2_level=5, p1_tiles=NT, p2_tiles=NT, parts='ABCD'):
    nc = bass.Bass("TRN2", target_bir_lowering=False)

    xT = nc.dram_tensor("xT", [D, NT * T], F32, kind="ExternalInput").ap()
    wq_d = nc.dram_tensor("wq", [D, D], F32, kind="ExternalInput").ap()
    wk_d = nc.dram_tensor("wk", [D, D], F32, kind="ExternalInput").ap()
    wv_d = nc.dram_tensor("wv", [D, D], F32, kind="ExternalInput").ap()
    wu_d = nc.dram_tensor("wu", [D, D], F32, kind="ExternalInput").ap()
    wo_d = nc.dram_tensor("wo", [D, D], F32, kind="ExternalInput").ap()
    w1tb = nc.dram_tensor("w1tb", [8, 128, 4096], F32, kind="ExternalInput").ap()
    w2tb = nc.dram_tensor("w2tb", [8, 128, 4096], F32, kind="ExternalInput").ap()
    cstf_d = nc.dram_tensor("cst_f", [128, 83], F32, kind="ExternalInput").ap()
    cstb_d = nc.dram_tensor("cst_b", [128, 401], BF16, kind="ExternalInput").ap()
    cstr_d = nc.dram_tensor("cst_r", [1, 2048], F32, kind="ExternalInput").ap()

    outT = nc.dram_tensor("outT", [D, NT * T], F32, kind="ExternalOutput").ap()
    w1c = nc.dram_tensor("w1c", [8, 128, 4096], BF16, kind="Internal").ap()
    w2c = nc.dram_tensor("w2c", [8, 128, 4096], BF16, kind="Internal").ap()
    w1f = nc.dram_tensor("w1f", [8, 128, 4096], BF16, kind="Internal").ap()
    w2f = nc.dram_tensor("w2f", [8, 128, 4096], BF16, kind="Internal").ap()
    lnx = nc.dram_tensor("lnx", [NT, 128, NCK * T], BF16, kind="Internal").ap()

    dbg_out = {}
    if dbg:
        for nm, shape in [("dbg_ln1", [128, D]), ("dbg_kv", [128, 512]),
                          ("dbg_q", [128, D]), ("dbg_attn", [128, D]),
                          ("dbg_x1", [128, D]), ("dbg_ln3", [128, D]),
                          ("dbg_h", [128, 512]), ("dbg_t", [4, T]),
                          ("dbg_u2", [16, T])]:
            dbg_out[nm] = nc.dram_tensor(nm, shape, F32, kind="ExternalOutput").ap()

    A = nc.alloc_sbuf_tensor
    wq_sb = A("wq_sb", [128, 8 * D], BF16)
    wu_sb = A("wu_sb", [128, 8 * D], BF16)
    wo_sb = A("wo_sb", [128, 8 * D], BF16)
    x_sb = A("x_sb", [128, NCK * T], F32)
    xb_sh = A("xb_sh", [128, NCK * T], BF16)
    ln_sh = A("ln_sh", [128, NCK * T], BF16)
    q_sh = A("q_sh", [128, NCK * T], BF16)
    x1_sb = A("x1_sb", [128, NCK * T], F32)
    t1_sb = A("t1_sb", [128, T], F32)
    sq_sb = A("sq_sb", [128, 2 * T], BF16)
    kvbd = A("kvbd", [128, 1024], BF16)
    cstf = A("cstf", [128, 83], F32)
    cstb = A("cstb", [128, 401], BF16)
    bqs = cstf[:, 0:8]
    bus = cstf[:, 8:16]
    bos = cstf[:, 16:24]
    b2s = cstf[:, 24:32]
    lnws = cstf[:, 32:40]
    lnbs = cstf[:, 40:48]
    b1s = cstf[:, 48:80]
    scale_s = cstf[0:1, 80:81]
    eps_c = cstf[0:1, 81:82]
    neghalf_c = cstf[0:1, 82:83]
    ident = cstb[:, 0:128]
    oh16 = cstb[0:1, 128:384]
    oh4 = cstb[0:1, 384:400]
    mean_r = A("mean_r", [1, T], F32)
    var_r = A("var_r", [1, T], F32)
    bm_r = A("bm_r", [1, T], F32)
    ones_c = A("ones_c", [128, 1], BF16)
    ones_r = A("ones_r", [1, 128], F32)
    ones_bsub = A("ones_bsub", [1, 128], BF16)
    psum = nc.alloc_psum_tensor("psum", [128, 4096], F32)

    def bank(b, p0=0, pn=128, c0=0, cn=512):
        return psum[p0:p0 + pn, b * 512 + c0:b * 512 + c0 + cn]

    em = Em(nc)
    for s in ("sP", "sV", "sA", "dSm", "dK", "dC", "dX", "dW1a", "dW1b",
              "dW2a", "dW2b", "dOa", "dOb", "dDbg", "dLs", "dLn",
              "dF1", "dF2s", "dF2", "dXg", "dLg"):
        em.sem(s)

    bufs = {n: Buf() for n in
            ("x", "xb", "ln", "q", "x1", "h", "t1", "sq", "ktm", "vtm",
             "w1b0", "w1b1", "w2b0", "w2b1", "outc0", "outc1",
             "t_sb", "u2_sb", "rows", "stg0", "stg1", "oht")}
    pbank = [Buf() for _ in range(8)]
    tks = {}

    def lnv(c):
        return ln_sh[:, c * T:(c + 1) * T]

    def xf(c):
        return x_sb[:, c * T:(c + 1) * T]

    def x1f(c):
        return x1_sb[:, c * T:(c + 1) * T]

    def xbv(c):
        return xb_sh[:, c * T:(c + 1) * T]

    def ln_stats(src_tok_fn):
        mm1 = [None, None]
        sqw = bufs["sq"].write_waits()
        for c in range(NCK):
            tsrc = src_tok_fn(c)
            par = c % 2
            sqv = sq_sb[:, par * T:(par + 1) * T]
            wv = [tsrc, mm1[par]] + (sqw if c == 0 else [])
            ta = em.op("scalar", lambda e, c=c, sqv=sqv: e.activation(
                sqv, xbv(c), AF.Square), wv, "sA")
            w0 = ([tsrc] + ([tks["oc"]] + pbank[0].write_waits() if c == 0 else []))
            em.op("tensor", lambda e, c=c: e.matmul(
                bank(0, 0, 1), ones_c[:], xbv(c),
                start=(c == 0), stop=(c == NCK - 1), skip_group_check=True), w0)
            w1 = [ta] + (pbank[1].write_waits() if c == 0 else [])
            tm = em.op("tensor", lambda e, c=c, sqv=sqv: e.matmul(
                bank(1, 0, 1), ones_c[:], sqv,
                start=(c == 0), stop=(c == NCK - 1), skip_group_check=True),
                w1, "sP")
            mm1[par] = tm
        tp = tm
        pbank[0].wrote(tp)
        pbank[1].wrote(tp)
        bufs["sq"].wrote(tp)
        bufs["sq"].read(tp)
        return tp

    def ln_rows_bcast(pe_stats, defer_rstd=False):
        rw = bufs["rows"].write_waits()
        tmean = em.op("vector", lambda e: e.tensor_scalar_mul(
            mean_r[:], bank(0, 0, 1), 1.0 / D), [pe_stats] + rw)
        w3 = [tmean, tks["or"]] + pbank[3].write_waits()
        tb3 = em.op("tensor", lambda e: e.matmul(bank(3), ones_r[:], mean_r[:],
                                                 start=True, stop=True), w3, "sP")
        pbank[3].wrote(tb3)
        trd = em.op("vector", lambda e: e.tensor_scalar_mul(var_r[:], bank(1, 0, 1), 1.0 / D),
                    [pe_stats], "sV")
        pbank[0].read(trd)
        pbank[1].read(trd)
        em.op("vector", lambda e: e.tensor_mul(bm_r[:], mean_r[:], mean_r[:]), [])
        tvar = em.op("vector", lambda e: e.tensor_sub(var_r[:], var_r[:], bm_r[:]),
                     [], "sV")
        em.op("scalar", lambda e: e.activation(var_r[:], var_r[:], AF.Ln,
                                               bias=eps_c), [tvar], "sA")
        trow = em.op("scalar", lambda e: e.activation(var_r[:], var_r[:], AF.Exp,
                                                      scale=neghalf_c), [], "sA")
        bufs["rows"].wrote(trow)

        def emit_b2():
            w2 = [trow, tks["or"]] + pbank[2].write_waits()
            tb2 = em.op("tensor", lambda e: e.matmul(bank(2), ones_r[:], var_r[:],
                                                     start=True, stop=True),
                        w2, "sP")
            pbank[2].wrote(tb2)
            bufs["rows"].read(tb2)
            return tb2

        if defer_rstd:
            return tb3, emit_b2
        return tb3, emit_b2()

    def ln_apply(src_f32, dst_bf, tb, first_waits):
        tb3, tb2 = tb
        toks = []
        for c in range(NCK):
            wv = [tb3] + (bufs["t1"].write_waits() if c == 0 else [])
            em.op("vector", lambda e, c=c: e.tensor_sub(t1_sb[:], src_f32(c), bank(3)), wv)
            wv2 = [tb2] + (list(first_waits) if c == 0 else [])
            tv = em.op("vector", lambda e, c=c: e.tensor_mul(dst_bf(c), t1_sb[:], bank(2)),
                       wv2, "sV")
            toks.append(tv)
        bufs["t1"].wrote(toks[-1])
        pbank[2].read(toks[-1])
        pbank[3].read(toks[-1])
        return toks

    def ln_full(src_f32, src_toks, dst_bf, dst_first_waits):
        xb_toks = []

        def prod(c):
            wv = [src_toks[c]] + (bufs["xb"].write_waits() if c == 0 else [])
            tv = em.op("scalar", lambda e, c=c: e.copy(xbv(c), src_f32(c)),
                       wv, "sA")
            xb_toks.append(tv)
            return tv

        tp = ln_stats(prod)
        bufs["xb"].wrote(xb_toks[-1])
        tb = ln_rows_bcast(tp)
        return ln_apply(src_f32, dst_bf, tb, dst_first_waits)

    with (
        nc.sbuf_tensor("wk_sb", [128, 8 * D], BF16) as wk_sb,
        nc.sbuf_tensor("wv_sb", [128, 8 * D], BF16) as wv_sb,
        nc.sbuf_tensor("k_tm", [128, D], BF16) as k_tm,
        nc.sbuf_tensor("v_tm", [128, D], BF16) as v_tm,
        nc.sbuf_tensor("kv_f", [128, 512], F32) as kv_f,
        nc.sbuf_tensor("kv_acc", [128, 1024], F32) as kv_acc,
        nc.sbuf_tensor("sign_f", [128, 512], F32) as sign_f,
        nc.sbuf_tensor("bkr_bf", [1, D], BF16) as bkr_bf,
        nc.sbuf_tensor("bvr_bf", [1, D], BF16) as bvr_bf,
        nc.sbuf_tensor("cstr", [1, 2048], F32) as cstr,
    ):
        def cast_proj(dst_sb, src):
            return em.dma("gpsimd", lambda e, dst_sb=dst_sb, src=src: e.dma_start(
                out=dst_sb.ap().rearrange("p (c d) -> p c d", d=D),
                in_=src.rearrange("(c p) d -> p c d", p=128)), [], "dC")

        if 'A' in parts:
            tk_wk = cast_proj(wk_sb, wk_d)
            tk_wv = cast_proj(wv_sb, wv_d)
            tk_wq = cast_proj(wq_sb, wq_d)
            tk_wu = cast_proj(wu_sb, wu_d)
            tk_wo = cast_proj(wo_sb, wo_d)
            tk_w1 = em.dma("gpsimd", lambda e: e.dma_start(out=w1c, in_=w1tb), [], "dC")
            tk_w2 = em.dma("gpsimd", lambda e: e.dma_start(out=w2c, in_=w2tb), [], "dC")
        else:
            tk_wk = tk_wv = tk_wq = tk_wu = tk_wo = tk_w1 = tk_w2 = None

        em.dma("sync", lambda e: e.dma_start(out=cstf[:], in_=cstf_d), [], "dSm")
        tk_cst = em.dma("sync", lambda e: e.dma_start(out=cstb[:], in_=cstb_d),
                        [], "dSm")
        tk_cr = em.dma("sync", lambda e: e.dma_start(out=cstr[:], in_=cstr_d),
                       [], "dK")
        tks["cst"] = tk_cst
        tks["oc"] = em.op("vector", lambda e: e.memset(ones_c[:], 1.0), [], "sV")
        tks["or"] = em.op("vector", lambda e: e.memset(ones_r[:], 1.0), [], "sV")
        tks["ob"] = em.op("vector", lambda e: e.memset(ones_bsub[:], 1.0), [], "sV")
        em.op("scalar", lambda e: e.copy(bkr_bf[0:1, :], cstr[0:1, 0:1024]),
              [tk_cr, tk_cst], "sA")
        tk_bias = em.op("scalar", lambda e: e.copy(bvr_bf[0:1, :],
                                                   cstr[0:1, 1024:2048]), [], "sA")
        tkva = em.op("vector", lambda e: e.memset(kv_acc[:], 0.0), [tk_cst], "sV")
        em.op("vector", lambda e: e.memset(kvbd[:], 0.0), [], "sV")


        for t in range(p1_tiles):
            xw = bufs["x"].write_waits()
            tx = em.dma("sync", lambda e, t=t: e.dma_start(
                out=x_sb.ap().rearrange("p (c s) -> p c s", s=T),
                in_=xT.rearrange("(c p) (t s) -> p t c s", p=128, s=T)[:, t, :, :]),
                xw, "dX")
            bufs["x"].wrote(tx)

            ap_toks = ln_full(xf, [tx] * NCK, lnv, bufs["ln"].write_waits())
            for c in range(NCK):
                bufs["x"].read(ap_toks[c])
            bufs["ln"].wrote(ap_toks[-1])
            tst = em.dma("sync", lambda e, t=t: e.dma_start(
                out=lnx[t], in_=ln_sh[:, :]), [ap_toks[-1]], "dLs")
            bufs["ln"].read(tst)

            for u in range(NSUB):
                evs = {}
                for (w_sb, brow, dst, dst_name, do_relu, wtok) in (
                        (wk_sb, bkr_bf, k_tm, "ktm", True, tk_wk),
                        (wv_sb, bvr_bf, v_tm, "vtm", False, tk_wv)):
                    for half in range(2):
                        b = 4 + half
                        wv0 = pbank[b].write_waits() + [wtok]
                        if u == 0:
                            wv0.append(ap_toks[0])
                        for k in range(NCK):
                            wk_ = (wv0 if k == 0 else
                                   ([ap_toks[k]] if u == 0 else []))
                            em.op("tensor",
                                  lambda e, u=u, k=k, half=half, b=b, w_sb=w_sb: e.matmul(
                                      bank(b),
                                      ln_sh[:, k * T + u * 128:k * T + (u + 1) * 128],
                                      w_sb[:, k * D + half * 512:k * D + (half + 1) * 512],
                                      start=(k == 0), stop=False,
                                      skip_group_check=True),
                                  wk_)
                        em.op("tensor", lambda e, b=b, half=half, brow=brow: e.matmul(
                            bank(b), ones_bsub[:],
                            brow[0:1, half * 512:(half + 1) * 512],
                            start=False, stop=True, skip_group_check=True),
                            [tk_bias, tks["ob"]])
                        tp = em.retro_inc("tensor", "sP")
                        pbank[b].wrote(tp)
                        wv1 = [tp] + bufs[dst_name].write_waits()
                        if do_relu:
                            tev = em.op("vector",
                                        lambda e, half=half, b=b, dst=dst: e.tensor_scalar_max(
                                            dst[:, half * 512:(half + 1) * 512],
                                            bank(b), 0.0), wv1, "sV")
                        else:
                            tev = em.op("scalar",
                                        lambda e, half=half, b=b, dst=dst: e.copy(
                                            dst[:, half * 512:(half + 1) * 512],
                                            bank(b)), wv1, "sA")
                        pbank[b].read(tev)
                        bufs[dst_name].wrote(tev)
                        evs[(dst_name, half)] = tev
                for b, h in ((6, 0), (7, 1)):
                    for j in range(4):
                        p = h * 4 + j
                        slot = j * 128
                        wv = ([evs[("ktm", h)], evs[("vtm", h)], tkva]
                              + pbank[b].write_waits()) if j == 0 else []
                        em.op("tensor",
                              lambda e, p=p, b=b, slot=slot: e.matmul(
                                  bank(b, 0, 128, slot, 128),
                                  k_tm[:, p * 128:(p + 1) * 128],
                                  v_tm[:, p * 128:(p + 1) * 128],
                                  start=True, stop=True, skip_group_check=True),
                              wv)
                    tkv = em.retro_inc("tensor", "sP")
                    bufs["ktm"].read(tkv)
                    bufs["vtm"].read(tkv)
                    pbank[b].wrote(tkv)
                    tka = em.op("vector", lambda e, b=b, h=h: e.tensor_add(
                        kv_acc[:, h * 512:(h + 1) * 512],
                        kv_acc[:, h * 512:(h + 1) * 512], bank(b)), [tkv], "sV")
                    pbank[b].read(tka)

        pe_p1_end = em.tok("sP")

        for p in (range(8) if 'D' in parts else []):
            off = (p // 4) * 512 + (p % 4) * 128
            em.op("vector", lambda e, p=p, off=off: e.tensor_copy(
                kv_f[0:64, p * 64:p * 64 + 64],
                kv_acc[0:64, off:off + 64]), [])
            em.op("vector", lambda e, p=p, off=off: e.tensor_copy(
                kv_f[64:128, p * 64:p * 64 + 64],
                kv_acc[64:128, off + 64:off + 128]), [])
        if 'D' in parts:
            tcp = em.op("vector", lambda e: e.engine_nop(), [], "sV")
            tsg = em.op("scalar", lambda e: e.activation(sign_f[:], kv_f[:], AF.Sign),
                        [tcp], "sA")
            tab = em.op("scalar", lambda e: e.activation(kv_f[:], kv_f[:], AF.Abs),
                        [], "sA")
            em.op("vector", lambda e: e.tensor_scalar_max(kv_f[:], kv_f[:], 0.01), [tab])
            em.op("vector", lambda e: e.tensor_scalar_min(kv_f[:], kv_f[:], 100.0), [])
            em.op("vector", lambda e: e.tensor_mul(kv_f[:], kv_f[:], sign_f[:]), [])
            tclamp = None
            for p in range(8):
                em.op("vector", lambda e, p=p: e.tensor_copy(
                    kvbd[0:64, p * 128:p * 128 + 64],
                    kv_f[0:64, p * 64:p * 64 + 64]), [])
                tclamp = em.op("vector", lambda e, p=p: e.tensor_copy(
                    kvbd[64:128, p * 128 + 64:p * 128 + 128],
                    kv_f[64:128, p * 64:p * 64 + 64]), [], "sV")
        else:
            tclamp = None

        if dbg:
            em.dma("gpsimd", lambda e: e.dma_start(out=dbg_out["dbg_kv"], in_=kv_f[:]),
                   [tclamp], "dDbg")

        scopeA_done = [tclamp, em.tok("sP"), em.tok("sV"), em.tok("sA"),
                       em.tok("dDbg")]
        em.flush("blk1")

    import contextlib
    with contextlib.ExitStack() as _st:
        def SB(name, shape, dt):
            return _st.enter_context(nc.sbuf_tensor(name, shape, dt))
        h_sb = SB("h_sb", [128, NHC * T], BF16)
        w1blk = SB("w1blk", [128, 2 * 4096], BF16)
        w2blk = SB("w2blk", [128, 2 * 4096], BF16)
        that4 = SB("that4", [4, 4096], BF16)
        ehat = SB("ehat", [16, 1024], BF16)
        ohat = SB("ohat", [128, 512], BF16)
        odd1g = SB("odd1g", [128, 32], BF16)
        row128 = SB("row128", [1, 128], BF16)
        t_sb = SB("t_sb", [4, T], BF16)
        u2_sb = SB("u2_sb", [16, T], BF16)
        outc = SB("outc", [128, 2 * T], F32)
        o4 = SB("o4", [4, 1024], BF16)
        ohT = SB("ohT", [16, 512], BF16)
        stage2 = SB("stage2", [128, 2048], BF16)
        e2rows = [h_sb[0:1, m * 1024:(m + 1) * 1024] for m in range(4)]

        p2x = {}
        p2ln = {}
        fw1_tok = {}
        fw2_tok = {}

        def emit_ln1_load(t, eng, extra):
            tk = em.dma(eng, lambda e, t=t: e.dma_start(
                out=ln_sh[:, :], in_=lnx[t]),
                bufs["ln"].write_waits() + extra,
                "dLn" if eng == "sync" else "dLg")
            bufs["ln"].wrote(tk)
            p2ln[t] = tk

        def emit_x_load(t, eng):
            tk = em.dma(eng, lambda e, t=t: e.dma_start(
                out=x_sb.ap().rearrange("p (c s) -> p c s", s=T),
                in_=xT.rearrange("(c p) (t s) -> p t c s", p=128, s=T)[:, t, :, :]),
                bufs["x"].write_waits(), "dX" if eng == "sync" else "dXg")
            bufs["x"].wrote(tk)
            p2x[t] = tk

        for t in range(p2_tiles):
            if t == 0:
                emit_ln1_load(0, "sync", [em.tok("dLs"), pe_p1_end])
                emit_x_load(0, "sync")
            tln1 = p2ln[t]
            tx = p2x[t]

            q_toks = []
            for j in range(NCK):
                b = 4 + (j % 2)
                wv = pbank[b].write_waits() + [tk_wq, tln1]
                for k in range(NCK):
                    em.op("tensor", lambda e, j=j, k=k, b=b: e.matmul(
                        bank(b), wq_sb[:, k * D + j * 128:k * D + (j + 1) * 128],
                        lnv(k), start=(k == 0), stop=(k == NCK - 1),
                        skip_group_check=True),
                        wv if k == 0 else [])
                tp = em.retro_inc("tensor", "sP")
                pbank[b].wrote(tp)
                tq = em.op("scalar", lambda e, j=j, b=b: e.activation(
                    q_sh[:, j * T:(j + 1) * T], bank(b), AF.Relu,
                    bias=bqs[:, j:j + 1]), [tp], "sA")
                pbank[b].read(tq)
                q_toks.append(tq)
            bufs["q"].wrote(q_toks[-1])
            if dbg and t == 0:
                bufs["q"].read(em.dma("gpsimd", lambda e: e.dma_start(
                    out=dbg_out["dbg_q"].rearrange("p (c s) -> p c s", s=128),
                    in_=q_sh.ap().rearrange("p (c s) -> p c s", s=T)[:, :, 0:128]),
                    [q_toks[-1]], "dDbg"))

            attn_toks = []
            xbw = bufs["xb"].write_waits()
            for j in range(NCK):
                b = 6 + (j % 2)
                wv = pbank[b].write_waits() + [tclamp, q_toks[j]]
                em.op("tensor", lambda e, j=j, b=b: e.matmul(
                    bank(b), kvbd[:, j * 128:(j + 1) * 128],
                    q_sh[:, j * T:(j + 1) * T],
                    start=True, stop=True, skip_group_check=True), wv)
                tp = em.retro_inc("tensor", "sP")
                pbank[b].wrote(tp)
                tev = em.op("scalar", lambda e, j=j, b=b: e.copy(
                    xbv(j), bank(b)), [tp] + (xbw if j == 0 else []), "sA")
                pbank[b].read(tev)
                attn_toks.append(tev)
            bufs["xb"].wrote(attn_toks[-1])
            if dbg and t == 0:
                bufs["xb"].read(em.dma("gpsimd", lambda e: e.dma_start(
                    out=dbg_out["dbg_attn"].rearrange("p (c s) -> p c s", s=128),
                    in_=xb_sh.ap().rearrange("p (c s) -> p c s", s=T)[:, :, 0:128]),
                    [attn_toks[-1]], "dDbg"))

            if p2_level < 2:
                continue
            tp2 = ln_stats(lambda c: attn_toks[c])
            tb2m, emit_b2 = ln_rows_bcast(tp2, defer_rstd=True)
            tb2r = None

            prod_toks = []
            for c in range(NCK):
                b = 4 + (c % 4)
                wv = pbank[b].write_waits() + [tk_wu]
                if c == 0:
                    wv.append(tln1)
                for k in range(NCK):
                    em.op("tensor", lambda e, c=c, k=k, b=b: e.matmul(
                        bank(b), wu_sb[:, k * D + c * 128:k * D + (c + 1) * 128],
                        lnv(k), start=(k == 0), stop=(k == NCK - 1),
                        skip_group_check=True), wv if k == 0 else [])
                tp = em.retro_inc("tensor", "sP")
                if c == 0:
                    tb2r = emit_b2()
                pbank[b].wrote(tp)
                em.op("vector", lambda e, c=c: e.tensor_sub(
                    t1_sb[:], xbv(c), bank(3)),
                    [tb2m] + bufs["t1"].write_waits())
                tv2_ = em.op("vector", lambda e: e.tensor_mul(t1_sb[:], t1_sb[:], bank(2)),
                             [tb2r], "sV")
                bufs["t1"].wrote(tv2_)
                ts2 = em.op("scalar", lambda e, c=c: e.activation(
                    t1_sb[:], t1_sb[:], AF.Identity,
                    bias=lnbs[:, c:c + 1], scale=lnws[:, c:c + 1]), [tv2_], "sA")
                tpr = em.op("vector", lambda e, c=c, b=b: e.scalar_tensor_tensor(
                    q_sh[:, c * T:(c + 1) * T], bank(b), bus[:, c:c + 1], t1_sb[:],
                    ALU.add, ALU.mult),
                    [tp, ts2] + (bufs["q"].write_waits() if c == 0 else []), "sV")
                bufs["t1"].wrote(tpr)
                pbank[b].read(tpr)
                pbank[2].read(tpr)
                pbank[3].read(tpr)
                prod_toks.append(tpr)
            bufs["q"].wrote(prod_toks[-1])

            x1w = bufs["x1"].write_waits()
            x1_toks = []
            for j in range(NCK):
                b = 6 + (j % 2)
                wv = pbank[b].write_waits() + [tk_wo]
                for k in range(NCK):
                    em.op("tensor", lambda e, j=j, k=k, b=b: e.matmul(
                        bank(b), wo_sb[:, k * D + j * 128:k * D + (j + 1) * 128],
                        q_sh[:, k * T:(k + 1) * T], start=(k == 0),
                        stop=(k == NCK - 1), skip_group_check=True),
                        (wv + [prod_toks[k]]) if k == 0 or j == 0 else [])
                tp = em.retro_inc("tensor", "sP")
                pbank[b].wrote(tp)
                tev = em.op("vector", lambda e, j=j, b=b: e.scalar_tensor_tensor(
                    x1f(j), bank(b), bos[:, j:j + 1], xf(j), ALU.add, ALU.add),
                    [tp, tx] + (x1w if j == 0 else []), "sV")
                pbank[b].read(tev)
                bufs["x"].read(tev)
                x1_toks.append(tev)
            bufs["x1"].wrote(x1_toks[-1])
            if t + 1 < p2_tiles:
                emit_x_load(t + 1, "sync" if t == 0 else "gpsimd")
            if dbg and t == 0:
                bufs["x1"].read(em.dma("gpsimd", lambda e: e.dma_start(
                    out=dbg_out["dbg_x1"].rearrange("p (c s) -> p c s", s=128),
                    in_=x1_sb.ap().rearrange("p (c s) -> p c s", s=T)[:, :, 0:128]),
                    [x1_toks[-1]], "dDbg"))

            if p2_level < 3:
                continue
            ln_war = em.tok("sP")
            ap3 = ln_full(x1f, x1_toks, lnv,
                          bufs["ln"].write_waits() + [ln_war])
            for c in range(NCK):
                bufs["x1"].read(ap3[c])
            bufs["ln"].wrote(ap3[-1])
            if dbg and t == 0:
                bufs["ln"].read(em.dma("gpsimd", lambda e: e.dma_start(
                    out=dbg_out["dbg_ln3"].rearrange("p (c s) -> p c s", s=128),
                    in_=ln_sh.ap().rearrange("p (c s) -> p c s", s=T)[:, :, 0:128]),
                    [ap3[-1]], "dDbg"))

            if t == 0:
                for k in range(NCK):
                    em.op("vector", lambda e, k=k: e.tensor_copy(
                        odd1g[:, k * 4:(k + 1) * 4],
                        ln_sh.ap()[:, k * T + 2:k * T + 9:2]),
                        [ap3[-1]] if k == 0 else [])
                em.op("vector", lambda e: e.memset(ohat[:], 0.0), [])
                for jc in range(NHC):
                    r, k = jc // 8, jc % 8
                    em.op("vector", lambda e, jc=jc, r=r, k=k: e.tensor_copy(
                        ohat[:, jc * 16 + 4 * r:jc * 16 + 4 * r + 4],
                        ln_sh.ap()[:, k * T + 10:k * T + 17:2]), [], "sV")
                for c in range(NHC):
                    i, k = c // 8, c % 8
                    wv = pbank[4].write_waits()
                    tp1 = em.op("tensor", lambda e, i=i, k=k: e.transpose(
                        bank(4, 0, 1, 0, 64).bitcast(BF16),
                        ln_sh[:, k * T + 1 + 2 * i:k * T + 2 + 2 * i], ident[:]),
                        wv + [ap3[-1], tks["cst"]], "sP")
                    pbank[4].wrote(tp1)
                    tv1 = em.op("vector", lambda e: e.tensor_scalar_mul(
                        row128[:], bank(4, 0, 1, 0, 64).bitcast(BF16),
                        scale_s[0:1, 0:1]), [tp1], "sV")
                    pbank[4].read(tv1)
                    wv = pbank[5].write_waits()
                    tp2_ = em.op("tensor", lambda e, i=i: e.matmul(
                        bank(5, 0, 4, 0, 128), oh4[0:1, i * 4:(i + 1) * 4],
                        row128[:], start=True, stop=True, skip_group_check=True),
                        wv + [tv1], "sP")
                    pbank[5].wrote(tp2_)
                    tv2_ = em.op("vector", lambda e, c=c: e.tensor_copy(
                        that4[0:4, c * 128:(c + 1) * 128], bank(5, 0, 4, 0, 128)),
                        [tp2_], "sV")
                    pbank[5].read(tv2_)
                for m in range(4):
                    for k in range(NCK):
                        wv = pbank[4].write_waits()
                        tp1 = em.op("tensor", lambda e, m=m, k=k: e.transpose(
                            bank(4, 0, 1, 0, 64).bitcast(BF16),
                            ln_sh[:, k * T + 9 + 2 * m:k * T + 10 + 2 * m],
                            ident[:]), wv, "sP")
                        pbank[4].wrote(tp1)
                        tv1 = em.op("vector", lambda e, m=m, k=k: e.tensor_copy(
                            e2rows[m][0:1, k * 128:(k + 1) * 128],
                            bank(4, 0, 1, 0, 64).bitcast(BF16)), [tp1], "sV")
                        pbank[4].read(tv1)
                te2 = None
                for m in range(4):
                    te2 = em.op("vector", lambda e, m=m: e.tensor_scalar_mul(
                        e2rows[m][:], e2rows[m][:], scale_s[0:1, 0:1]), [], "sV")
                for c in range(NCK):
                    m = c // 2
                    wv = pbank[5].write_waits()
                    for r in range(4):
                        rho = 4 * r + m
                        st = 512 * (c % 2) + r
                        em.op("tensor",
                              lambda e, m=m, rho=rho, st=st, r=r: e.matmul(
                                  bank(5, 0, 16, 0, 128),
                                  oh16[0:1, rho * 16:(rho + 1) * 16],
                                  e2rows[m][0:1, st:st + 509:4],
                                  start=(r == 0), stop=(r == 3),
                                  skip_group_check=True),
                              (wv + [te2]) if r == 0 else [])
                    tp = em.retro_inc("tensor", "sP")
                    pbank[5].wrote(tp)
                    tv = em.op("vector", lambda e, c=c: e.tensor_copy(
                        ehat[0:16, c * 128:(c + 1) * 128], bank(5, 0, 16, 0, 128)),
                        [tp], "sV")
                    pbank[5].read(tv)
                bufs["t_sb"].wrote(tv)
                bufs["h"].read(tp)
                t_o1g = em.tok("sV")
                for k in range(NCK):
                    wv = pbank[4].write_waits()
                    tp1 = em.op("tensor", lambda e, k=k: e.transpose(
                        bank(4, 0, 4, 0, 64).bitcast(BF16),
                        odd1g[:, k * 4:(k + 1) * 4], ident[:]),
                        wv + [t_o1g], "sP")
                    pbank[4].wrote(tp1)
                    tv1 = em.op("vector", lambda e, k=k: e.tensor_copy(
                        o4[0:4, k * 128:(k + 1) * 128],
                        bank(4, 0, 4, 0, 64).bitcast(BF16)), [tp1], "sV")
                    pbank[4].read(tv1)
                t_mach = em.tok("sV")

            if t == 0:
                wv = pbank[0].write_waits() + [bufs["t_sb"].writer, ap3[0]]
                for k in range(NCK):
                    em.op("tensor", lambda e, k=k: e.matmul(
                        bank(0, 0, 4), odd1g[:, k * 4:(k + 1) * 4], lnv(k),
                        start=(k == 0), stop=(k == NCK - 1),
                        skip_group_check=True),
                        wv if k == 0 else [ap3[k]])
                tpt = em.retro_inc("tensor", "sP")
                pbank[0].wrote(tpt)
                ttv = em.op("vector", lambda e: e.tensor_copy(t_sb[:], bank(0, 0, 4)),
                            [tpt] + bufs["t_sb"].write_waits(), "sV")
                pbank[0].read(ttv)
                bufs["t_sb"].wrote(ttv)
                if dbg:
                    bufs["t_sb"].read(em.dma("gpsimd", lambda e: e.dma_start(
                        out=dbg_out["dbg_t"], in_=t_sb[:]), [ttv], "dDbg"))

            if p2_level < 4:
                continue
            hw = bufs["h"].write_waits()
            u2w = pbank[1].write_waits() if t == 0 else []
            h_toks = []
            w1_toks = [None] * 8

            def load_w1(jcb):
                wbuf = jcb % 2
                bname = "w1b%d" % wbuf
                src = w1c if t == 0 else w1f
                extra = [tk_w1] if t == 0 else [fw1_tok[jcb]]
                tw = em.dma("sync", lambda e, jcb=jcb, wbuf=wbuf, src=src: e.dma_start(
                    out=w1blk[:, wbuf * 4096:(wbuf + 1) * 4096], in_=src[jcb]),
                    bufs[bname].write_waits() + extra
                    + (scopeA_done if t == 0 and jcb < 2 else []),
                    "dW1a" if wbuf == 0 else "dW1b")
                w1_toks[jcb] = tw
                bufs[bname].wrote(tw)

            def fold_w1(jcb):
                wbuf = jcb % 2
                bname = "w1b%d" % wbuf
                rd = bufs[bname].write_waits()
                ta = None
                for k in range(NCK):
                    bf = 2 + (k % 2)
                    wvf = pbank[bf].write_waits() + ([t_mach] if k < 2 else [])
                    tm = em.op("tensor", lambda e, k=k, bf=bf, jcb=jcb: e.matmul(
                        bank(bf), o4[0:4, k * 128:(k + 1) * 128],
                        that4[0:4, jcb * 512:(jcb + 1) * 512],
                        start=True, stop=True, skip_group_check=True), wvf, "sP")
                    pbank[bf].wrote(tm)
                    ta = em.op("vector", lambda e, k=k, bf=bf, wbuf=wbuf: e.tensor_add(
                        w1blk[:, wbuf * 4096 + k * 512:wbuf * 4096 + (k + 1) * 512],
                        w1blk[:, wbuf * 4096 + k * 512:wbuf * 4096 + (k + 1) * 512],
                        bank(bf)), [tm] + (rd if k == 0 else []), "sV")
                    pbank[bf].read(ta)
                ts = em.dma("sync", lambda e, jcb=jcb, wbuf=wbuf: e.dma_start(
                    out=w1f[jcb], in_=w1blk[:, wbuf * 4096:(wbuf + 1) * 4096]),
                    [ta], "dF1")
                bufs[bname].read(ts)
                fw1_tok[jcb] = ts

            def fold_w2(jb):
                ohw = bufs["oht"].write_waits()
                cps = []
                for jsub in range(4):
                    jc = jb * 4 + jsub
                    bf = 6 + (jsub % 2)
                    wvf = pbank[bf].write_waits() + ([t_mach] if jsub < 2 else [])
                    tp1 = em.op("tensor", lambda e, jc=jc, bf=bf: e.transpose(
                        bank(bf, 0, 16, 0, 64).bitcast(BF16),
                        ohat[:, jc * 16:(jc + 1) * 16], ident[:]), wvf, "sP")
                    pbank[bf].wrote(tp1)
                    tcp = em.op("vector", lambda e, jsub=jsub, bf=bf: e.tensor_copy(
                        ohT[0:16, jsub * 128:(jsub + 1) * 128],
                        bank(bf, 0, 16, 0, 64).bitcast(BF16)),
                        [tp1] + (ohw if jsub == 0 else []), "sV")
                    pbank[bf].read(tcp)
                    cps.append(tcp)
                tls = {}

                def stage_load(jsub):
                    hb = jsub % 2
                    tls[jsub] = em.dma("gpsimd", lambda e, jb=jb, jsub=jsub, hb=hb:
                                       e.dma_start(
                                           out=stage2[:, hb * 1024:(hb + 1) * 1024],
                                           in_=w2c[jb][:, jsub * 1024:(jsub + 1) * 1024]),
                                       bufs["stg%d" % hb].write_waits() + [tk_w2],
                                       "dF2s")

                stage_load(0)
                stage_load(1)
                tm = None
                for jsub in range(4):
                    hb = jsub % 2
                    ta = None
                    for hh in range(2):
                        bf = 0 + hh
                        wvf = pbank[bf].write_waits() + [cps[jsub]]
                        tm = em.op("tensor",
                                   lambda e, jsub=jsub, bf=bf, hh=hh: e.matmul(
                                       bank(bf),
                                       ohT[0:16, jsub * 128:(jsub + 1) * 128],
                                       ehat[0:16, hh * 512:(hh + 1) * 512],
                                       start=True, stop=True,
                                       skip_group_check=True), wvf, "sP")
                        pbank[bf].wrote(tm)
                        ta = em.op("vector",
                                   lambda e, hb=hb, hh=hh, bf=bf: e.tensor_add(
                                       stage2[:, hb * 1024 + hh * 512:
                                              hb * 1024 + (hh + 1) * 512],
                                       stage2[:, hb * 1024 + hh * 512:
                                              hb * 1024 + (hh + 1) * 512],
                                       bank(bf)), [tm, tls[jsub]], "sV")
                        pbank[bf].read(ta)
                    ts = em.dma("sync", lambda e, jb=jb, jsub=jsub, hb=hb:
                                e.dma_start(
                                    out=w2f[jb][:, jsub * 1024:(jsub + 1) * 1024],
                                    in_=stage2[:, hb * 1024:(hb + 1) * 1024]),
                                [ta], "dF2")
                    bufs["stg%d" % hb].read(ts)
                    if jsub + 2 < 4:
                        stage_load(jsub + 2)
                bufs["oht"].read(tm)
                fw2_tok[jb] = ts

            load_w1(0)
            load_w1(1)
            for jcb in range(8):
                wbuf = jcb % 2
                bname = "w1b%d" % wbuf
                for js in range(4):
                    jc = jcb * 4 + js
                    b = 4 + (jc % 2)
                    wv = pbank[b].write_waits() + [w1_toks[jcb], ap3[0]]
                    if t == 0:
                        wv.append(bufs["t_sb"].writer)
                    for k in range(NCK):
                        em.op("tensor",
                              lambda e, js=js, k=k, b=b, wbuf=wbuf, t=t: e.matmul(
                                  bank(b),
                                  w1blk[:, wbuf * 4096 + k * 512 + js * 128:
                                        wbuf * 4096 + k * 512 + (js + 1) * 128],
                                  lnv(k), start=(k == 0),
                                  stop=(t > 0 and k == NCK - 1),
                                  skip_group_check=True),
                              wv if k == 0 else [ap3[k]])
                    if t == 0:
                        em.op("tensor", lambda e, jc=jc, b=b: e.matmul(
                            bank(b), that4[0:4, jc * 128:(jc + 1) * 128], t_sb[:],
                            start=False, stop=True, skip_group_check=True), [])
                    tp = em.retro_inc("tensor", "sP")
                    pbank[b].wrote(tp)
                    bufs[bname].read(tp)
                    if t == 0:
                        bufs["t_sb"].read(tp)
                    th = em.op("scalar", lambda e, jc=jc, b=b: e.activation(
                        h_sb[:, jc * T:(jc + 1) * T], bank(b), AF.Gelu,
                        bias=b1s[:, jc:jc + 1]),
                        [tp] + (hw if jc == 0 else []), "sA")
                    pbank[b].read(th)
                    h_toks.append(th)
                if t == 0 and p2_level >= 5:
                    fold_w1(jcb)
                    fold_w2(jcb)
                if jcb + 2 < 8:
                    load_w1(jcb + 2)
            bufs["h"].wrote(h_toks[-1])
            bufs["ln"].read(em.tok("sP"))
            if t + 1 < p2_tiles:
                emit_ln1_load(t + 1, "sync" if t == 0 else "gpsimd", [])
            if t == 0:
                u2w = u2w + pbank[1].write_waits()
                for jc in range(NHC):
                    em.op("tensor", lambda e, jc=jc: e.matmul(
                        bank(1, 0, 16), ohat[:, jc * 16:(jc + 1) * 16],
                        h_sb[:, jc * T:(jc + 1) * T],
                        start=(jc == 0), stop=(jc == NHC - 1),
                        skip_group_check=True),
                        ([h_toks[-1], bufs["t_sb"].writer] + u2w) if jc == 0 else [])
                tpu2 = em.retro_inc("tensor", "sP")
                pbank[1].wrote(tpu2)
                tu2 = em.op("vector", lambda e: e.tensor_copy(u2_sb[:], bank(1, 0, 16)),
                            [tpu2] + bufs["u2_sb"].write_waits(), "sV")
                pbank[1].read(tu2)
                bufs["u2_sb"].wrote(tu2)
            if dbg and t == 0:
                bufs["u2_sb"].read(em.dma("gpsimd", lambda e: e.dma_start(
                    out=dbg_out["dbg_u2"], in_=u2_sb[:]), [tu2], "dDbg"))
                bufs["h"].read(em.dma("gpsimd", lambda e: e.dma_start(
                    out=dbg_out["dbg_h"], in_=h_sb[:, 0:512]), [h_toks[-1]], "dDbg"))

            if p2_level < 5:
                continue
            w2_toks = [None] * 8

            def load_w2(jb):
                wbuf = jb % 2
                bname = "w2b%d" % wbuf
                src = w2c if t == 0 else w2f
                extra = [tk_w2] if t == 0 else [fw2_tok[jb]]
                tw = em.dma("sync", lambda e, jb=jb, wbuf=wbuf, src=src: e.dma_start(
                    out=w2blk[:, wbuf * 4096:(wbuf + 1) * 4096], in_=src[jb]),
                    bufs[bname].write_waits() + extra
                    + (scopeA_done if t == 0 and jb < 2 else []),
                    "dW2a" if wbuf == 0 else "dW2b")
                w2_toks[jb] = tw
                bufs[bname].wrote(tw)

            load_w2(0)
            load_w2(1)
            bank_wv = []
            for b in range(8):
                bank_wv += pbank[b].write_waits()
            for jb in range(8):
                wbuf = jb % 2
                bname = "w2b%d" % wbuf
                for jsub in range(4):
                    jc = jb * 4 + jsub
                    for c in range(8):
                        if jc == 0 and c == 0:
                            wv = bank_wv + [w2_toks[jb], h_toks[0]]
                        elif c == 0:
                            wv = [w2_toks[jb], h_toks[jc]]
                        else:
                            wv = []
                        em.op("tensor",
                              lambda e, jsub=jsub, c=c, jc=jc, wbuf=wbuf, t=t: e.matmul(
                                  bank(c),
                                  w2blk[:, wbuf * 4096 + jsub * 1024 + c * 128:
                                        wbuf * 4096 + jsub * 1024 + (c + 1) * 128],
                                  h_sb[:, jc * T:(jc + 1) * T],
                                  start=(jc == 0),
                                  stop=(t > 0 and jc == NHC - 1),
                                  skip_group_check=True), wv)
                    tp = em.retro_inc("tensor", "sP")
                    bufs[bname].read(tp)
                    bufs["h"].read(tp)
                if jb + 2 < 8:
                    load_w2(jb + 2)
            tp_last = tp
            for c in (4, 5, 6, 7, 0, 1, 2, 3):
                if t == 0:
                    em.op("tensor", lambda e, c=c: e.matmul(
                        bank(c), ehat[0:16, c * 128:(c + 1) * 128], u2_sb[:],
                        start=False, stop=True, skip_group_check=True),
                        [tu2] if c == 4 else [])
                    tp = em.retro_inc("tensor", "sP")
                    bufs["u2_sb"].read(tp)
                else:
                    tp = tp_last
                pbank[c].wrote(tp)
                ob = c % 2
                obname = "outc%d" % ob
                tev = em.op("vector", lambda e, c=c, ob=ob: e.scalar_tensor_tensor(
                    outc[:, ob * T:(ob + 1) * T], bank(c), b2s[:, c:c + 1],
                    x1f(c), ALU.add, ALU.add),
                    [tp, ap3[-1]] + bufs[obname].write_waits(), "sV")
                pbank[c].read(tev)
                bufs["x1"].read(tev)
                tdma = em.dma("gpsimd", lambda e, c=c, t=t, ob=ob: e.dma_start(
                    out=outT.rearrange("(c p) (t s) -> p c t s", p=128, s=T)[:, c, t, :],
                    in_=outc[:, ob * T:(ob + 1) * T]), [tev],
                    "dOa" if ob == 0 else "dOb")
                bufs[obname].wrote(tev)
                bufs[obname].read(tdma)

        em.op("gpsimd", lambda e: e.engine_nop(),
              [em.tok("dOa"), em.tok("dOb"), em.tok("dDbg")])
        em.flush("blk2")

    return nc


def _host_layout(inputs):
    q = inputs
    xs = np.ascontiguousarray(np.asarray(q["L0_attn_query"], dtype=np.float32))
    W1 = np.asarray(q["L0_ffn_W1"], dtype=np.float32)
    W2 = np.asarray(q["L0_ffn_W2"], dtype=np.float32)
    w1t = np.ascontiguousarray(W1.T)
    w2t = np.ascontiguousarray(W2.T)
    w1tb = np.ascontiguousarray(
        w1t.reshape(8, 128, 8, 512).transpose(2, 1, 0, 3)).reshape(8, 128, 4096)
    w2tb = np.ascontiguousarray(
        w2t.reshape(8, 4, 128, 1024).transpose(0, 2, 1, 3)).reshape(8, 128, 4096)

    def pp(v, n):
        return np.ascontiguousarray(np.asarray(v, dtype=np.float32).reshape(n, 128).T)

    cst_f = np.zeros((128, 83), dtype=np.float32)
    cst_f[:, 0:8] = pp(q["L0_attn_bq"], 8)
    cst_f[:, 8:16] = pp(q["L0_attn_bu"], 8)
    cst_f[:, 16:24] = pp(q["L0_attn_bo"], 8)
    cst_f[:, 24:32] = pp(q["L0_ffn_b2"], 8)
    cst_f[:, 32:40] = pp(q["L0_attn_ln_w"], 8)
    cst_f[:, 40:48] = pp(q["L0_attn_ln_b"], 8)
    cst_f[:, 48:80] = pp(q["L0_ffn_b1"], 32)
    cst_f[0, 80] = np.asarray(q["L0_ffn_scale"], dtype=np.float32).reshape(())
    cst_f[0, 81] = 1e-5
    cst_f[0, 82] = -0.5

    cst_b = np.zeros((128, 401), dtype=ml_dtypes.bfloat16)
    cst_b[:, 0:128] = np.eye(128, dtype=ml_dtypes.bfloat16)
    for rho in range(16):
        cst_b[0, 128 + rho * 16 + rho] = 1
    for i in range(4):
        cst_b[0, 384 + i * 4 + i] = 1

    cst_r = np.zeros((1, 2048), dtype=np.float32)
    cst_r[0, 0:1024] = np.asarray(q["L0_attn_bk"], dtype=np.float32)
    cst_r[0, 1024:2048] = np.asarray(q["L0_attn_bv"], dtype=np.float32)

    common = {
        "wq": np.ascontiguousarray(np.asarray(q["L0_attn_Wq"], dtype=np.float32)),
        "wk": np.ascontiguousarray(np.asarray(q["L0_attn_Wk"], dtype=np.float32)),
        "wv": np.ascontiguousarray(np.asarray(q["L0_attn_Wv"], dtype=np.float32)),
        "wu": np.ascontiguousarray(np.asarray(q["L0_attn_Wu"], dtype=np.float32)),
        "wo": np.ascontiguousarray(np.asarray(q["L0_attn_Wo"], dtype=np.float32)),
        "w1tb": w1tb, "w2tb": w2tb,
        "cst_f": cst_f, "cst_b": cst_b, "cst_r": cst_r,
    }
    return [dict(common, xT=np.ascontiguousarray(xs[i].T))
            for i in range(xs.shape[0])]


def kernel(**inputs):
    in_maps = _host_layout(inputs)
    nc = build_nc(dbg=DBG)
    res = run_bass_kernel_spmd(nc, in_maps, core_ids=list(range(8)))
    outs = [np.ascontiguousarray(np.asarray(r["outT"]).T) for r in res.results]
    return np.stack(outs, axis=0)


# version 18 (final)
# speedup vs baseline: 1.0138x; 1.0054x over previous
import numpy as np
import ml_dtypes
import concourse.bass as bass
import concourse.mybir as mybir
from concourse.bass_utils import run_bass_kernel_spmd

F32 = mybir.dt.float32
BF16 = mybir.dt.bfloat16
AF = mybir.ActivationFunctionType
ALU = mybir.AluOpType

D = 1024
NCK = 8
T = 512
NT = 8
NSUB = 4
NHC = 32
NH = 16
HD = 64
EPS = 1e-5
ENGINES = ("tensor", "vector", "scalar", "sync", "gpsimd")

DBG = False


class Em:

    def __init__(self, nc):
        self.nc = nc
        self.q = {e: [] for e in ENGINES}
        self.sems = {}
        self.watermark = {}

    def sem(self, name):
        self.sems[name] = [self.nc.alloc_semaphore(name), 0]

    AUTO = {"vector": "sV", "scalar": "sA"}

    def op(self, eng, fn, waits=(), inc=None, n=1):
        ws = [w for w in waits if w is not None and w[1] > 0]
        auto = self.AUTO.get(eng)
        if auto is not None:
            prev = self.sems[auto][1]
            if prev > 0:
                ws.append((auto, prev))
            inc = auto
            n = 1
        self.q[eng].append((ws, fn, inc, n))
        if inc is not None:
            self.sems[inc][1] += n
            return (inc, self.sems[inc][1])
        return None

    def retro_inc(self, eng, sem, n=1):
        ws, fn, inc, ni = self.q[eng][-1]
        assert inc is None, "last op already increments a semaphore"
        self.q[eng][-1] = (ws, fn, sem, n)
        self.sems[sem][1] += n
        return (sem, self.sems[sem][1])

    def dma(self, eng, fn, waits=(), inc=None):
        return self.op(eng, fn, list(waits) + [self.tok(inc)], inc, 16)

    def tok(self, name):
        return (name, self.sems[name][1])

    def flush(self, name):
        qs, self.q = self.q, {e: [] for e in ENGINES}
        nc = self.nc

        def mk(eng_name):
            def body(eng):
                for ws, fn, inc, n in qs[eng_name]:
                    need = []
                    for sname, val in ws:
                        key = (eng_name, sname)
                        if self.watermark.get(key, 0) < val:
                            need.append((sname, val))
                            self.watermark[key] = val
                    for sname, val in need[:-1]:
                        eng.wait_ge(self.sems[sname][0], val)
                    inst = fn(eng)
                    if need:
                        sname, val = need[-1]
                        inst._wait_ge(self.sems[sname][0], val)
                    if inc is not None:
                        inst.then_inc(self.sems[inc][0], n)
            return body

        with nc.Block(name) as block:
            for e in ENGINES:
                if qs[e]:
                    getattr(block, e)(mk(e))


class Buf:
    def __init__(self):
        self.writer = None
        self.readers = []

    def read(self, tok):
        if tok is not None:
            self.readers.append(tok)

    def write_waits(self):
        r, self.readers = self.readers, []
        return r

    def wrote(self, tok):
        self.writer = tok


def build_nc(dbg=False, p2_level=5, p1_tiles=NT, p2_tiles=NT, parts='ABCD'):
    nc = bass.Bass("TRN2", target_bir_lowering=False)

    xT = nc.dram_tensor("xT", [D, NT * T], F32, kind="ExternalInput").ap()
    wq_d = nc.dram_tensor("wq", [D, D], F32, kind="ExternalInput").ap()
    wk_d = nc.dram_tensor("wk", [D, D], F32, kind="ExternalInput").ap()
    wv_d = nc.dram_tensor("wv", [D, D], F32, kind="ExternalInput").ap()
    wu_d = nc.dram_tensor("wu", [D, D], F32, kind="ExternalInput").ap()
    wo_d = nc.dram_tensor("wo", [D, D], F32, kind="ExternalInput").ap()
    w1tb = nc.dram_tensor("w1tb", [8, 128, 4096], F32, kind="ExternalInput").ap()
    w2tb = nc.dram_tensor("w2tb", [8, 128, 4096], F32, kind="ExternalInput").ap()
    cstf_d = nc.dram_tensor("cst_f", [128, 83], F32, kind="ExternalInput").ap()
    cstb_d = nc.dram_tensor("cst_b", [128, 401], BF16, kind="ExternalInput").ap()
    cstr_d = nc.dram_tensor("cst_r", [1, 2048], F32, kind="ExternalInput").ap()

    outT = nc.dram_tensor("outT", [D, NT * T], F32, kind="ExternalOutput").ap()
    w1c = nc.dram_tensor("w1c", [8, 128, 4096], BF16, kind="Internal").ap()
    w2c = nc.dram_tensor("w2c", [8, 128, 4096], BF16, kind="Internal").ap()
    w1f = nc.dram_tensor("w1f", [8, 128, 4096], BF16, kind="Internal").ap()
    w2f = nc.dram_tensor("w2f", [8, 128, 4096], BF16, kind="Internal").ap()
    lnx = nc.dram_tensor("lnx", [NT, 128, NCK * T], BF16, kind="Internal").ap()

    dbg_out = {}
    if dbg:
        for nm, shape in [("dbg_ln1", [128, D]), ("dbg_kv", [128, 512]),
                          ("dbg_q", [128, D]), ("dbg_attn", [128, D]),
                          ("dbg_x1", [128, D]), ("dbg_ln3", [128, D]),
                          ("dbg_h", [128, 512]), ("dbg_t", [4, T]),
                          ("dbg_u2", [16, T])]:
            dbg_out[nm] = nc.dram_tensor(nm, shape, F32, kind="ExternalOutput").ap()

    A = nc.alloc_sbuf_tensor
    wq_sb = A("wq_sb", [128, 8 * D], BF16)
    wu_sb = A("wu_sb", [128, 8 * D], BF16)
    wo_sb = A("wo_sb", [128, 8 * D], BF16)
    x_sb = A("x_sb", [128, NCK * T], F32)
    xb_sh = A("xb_sh", [128, NCK * T], BF16)
    ln_sh = A("ln_sh", [128, NCK * T], BF16)
    q_sh = A("q_sh", [128, NCK * T], BF16)
    x1_sb = A("x1_sb", [128, NCK * T], F32)
    t1_sb = A("t1_sb", [128, T], F32)
    sq_sb = A("sq_sb", [128, 2 * T], BF16)
    kvbd = A("kvbd", [128, 1024], BF16)
    cstf = A("cstf", [128, 83], F32)
    cstb = A("cstb", [128, 401], BF16)
    bqs = cstf[:, 0:8]
    bus = cstf[:, 8:16]
    bos = cstf[:, 16:24]
    b2s = cstf[:, 24:32]
    lnws = cstf[:, 32:40]
    lnbs = cstf[:, 40:48]
    b1s = cstf[:, 48:80]
    scale_s = cstf[0:1, 80:81]
    eps_c = cstf[0:1, 81:82]
    neghalf_c = cstf[0:1, 82:83]
    ident = cstb[:, 0:128]
    oh16 = cstb[0:1, 128:384]
    oh4 = cstb[0:1, 384:400]
    mean_r = A("mean_r", [1, T], F32)
    var_r = A("var_r", [1, T], F32)
    bm_r = A("bm_r", [1, T], F32)
    ones_c = A("ones_c", [128, 1], BF16)
    ones_r = A("ones_r", [1, 128], F32)
    ones_bsub = A("ones_bsub", [1, 128], BF16)
    psum = nc.alloc_psum_tensor("psum", [128, 4096], F32)

    def bank(b, p0=0, pn=128, c0=0, cn=512):
        return psum[p0:p0 + pn, b * 512 + c0:b * 512 + c0 + cn]

    em = Em(nc)
    for s in ("sP", "sV", "sA", "dSm", "dK", "dC", "dX", "dW1a", "dW1b",
              "dW2a", "dW2b", "dOa", "dOb", "dDbg", "dLs", "dLn",
              "dF1", "dF2s", "dF2", "dXg", "dLg"):
        em.sem(s)

    bufs = {n: Buf() for n in
            ("x", "xb", "ln", "q", "x1", "h", "t1", "sq", "ktm", "vtm",
             "w1b0", "w1b1", "w2b0", "w2b1", "outc0", "outc1",
             "t_sb", "u2_sb", "rows", "stg0", "stg1", "oht")}
    pbank = [Buf() for _ in range(8)]
    tks = {}

    def lnv(c):
        return ln_sh[:, c * T:(c + 1) * T]

    def xf(c):
        return x_sb[:, c * T:(c + 1) * T]

    def x1f(c):
        return x1_sb[:, c * T:(c + 1) * T]

    def xbv(c):
        return xb_sh[:, c * T:(c + 1) * T]

    def ln_stats(src_tok_fn):
        mm1 = [None, None]
        sqw = bufs["sq"].write_waits()
        for c in range(NCK):
            tsrc = src_tok_fn(c)
            par = c % 2
            sqv = sq_sb[:, par * T:(par + 1) * T]
            wv = [tsrc, mm1[par]] + (sqw if c == 0 else [])
            ta = em.op("scalar", lambda e, c=c, sqv=sqv: e.activation(
                sqv, xbv(c), AF.Square), wv, "sA")
            w0 = ([tsrc] + ([tks["oc"]] + pbank[0].write_waits() if c == 0 else []))
            em.op("tensor", lambda e, c=c: e.matmul(
                bank(0, 0, 1), ones_c[:], xbv(c),
                start=(c == 0), stop=(c == NCK - 1), skip_group_check=True), w0)
            w1 = [ta] + (pbank[1].write_waits() if c == 0 else [])
            tm = em.op("tensor", lambda e, c=c, sqv=sqv: e.matmul(
                bank(1, 0, 1), ones_c[:], sqv,
                start=(c == 0), stop=(c == NCK - 1), skip_group_check=True),
                w1, "sP")
            mm1[par] = tm
        tp = tm
        pbank[0].wrote(tp)
        pbank[1].wrote(tp)
        bufs["sq"].wrote(tp)
        bufs["sq"].read(tp)
        return tp

    def ln_rows_bcast(pe_stats, defer_rstd=False):
        rw = bufs["rows"].write_waits()
        tmean = em.op("vector", lambda e: e.tensor_scalar_mul(
            mean_r[:], bank(0, 0, 1), 1.0 / D), [pe_stats] + rw)
        w3 = [tmean, tks["or"]] + pbank[3].write_waits()
        tb3 = em.op("tensor", lambda e: e.matmul(bank(3), ones_r[:], mean_r[:],
                                                 start=True, stop=True), w3, "sP")
        pbank[3].wrote(tb3)
        trd = em.op("vector", lambda e: e.tensor_scalar_mul(var_r[:], bank(1, 0, 1), 1.0 / D),
                    [pe_stats], "sV")
        pbank[0].read(trd)
        pbank[1].read(trd)
        em.op("vector", lambda e: e.tensor_mul(bm_r[:], mean_r[:], mean_r[:]), [])
        tvar = em.op("vector", lambda e: e.tensor_sub(var_r[:], var_r[:], bm_r[:]),
                     [], "sV")
        em.op("scalar", lambda e: e.activation(var_r[:], var_r[:], AF.Ln,
                                               bias=eps_c), [tvar], "sA")
        trow = em.op("scalar", lambda e: e.activation(var_r[:], var_r[:], AF.Exp,
                                                      scale=neghalf_c), [], "sA")
        bufs["rows"].wrote(trow)

        def emit_b2():
            w2 = [trow, tks["or"]] + pbank[2].write_waits()
            tb2 = em.op("tensor", lambda e: e.matmul(bank(2), ones_r[:], var_r[:],
                                                     start=True, stop=True),
                        w2, "sP")
            pbank[2].wrote(tb2)
            bufs["rows"].read(tb2)
            return tb2

        if defer_rstd:
            return tb3, emit_b2
        return tb3, emit_b2()

    def ln_apply(src_f32, dst_bf, tb, first_waits):
        tb3, tb2 = tb
        toks = []
        for c in range(NCK):
            wv = [tb3] + (bufs["t1"].write_waits() if c == 0 else [])
            em.op("vector", lambda e, c=c: e.tensor_sub(t1_sb[:], src_f32(c), bank(3)), wv)
            wv2 = [tb2] + (list(first_waits) if c == 0 else [])
            tv = em.op("vector", lambda e, c=c: e.tensor_mul(dst_bf(c), t1_sb[:], bank(2)),
                       wv2, "sV")
            toks.append(tv)
        bufs["t1"].wrote(toks[-1])
        pbank[2].read(toks[-1])
        pbank[3].read(toks[-1])
        return toks

    def ln_full(src_f32, src_toks, dst_bf, dst_first_waits):
        xb_toks = []

        def prod(c):
            wv = [src_toks[c]] + (bufs["xb"].write_waits() if c == 0 else [])
            tv = em.op("scalar", lambda e, c=c: e.copy(xbv(c), src_f32(c)),
                       wv, "sA")
            xb_toks.append(tv)
            return tv

        tp = ln_stats(prod)
        bufs["xb"].wrote(xb_toks[-1])
        tb = ln_rows_bcast(tp)
        return ln_apply(src_f32, dst_bf, tb, dst_first_waits)

    with (
        nc.sbuf_tensor("wk_sb", [128, 8 * D], BF16) as wk_sb,
        nc.sbuf_tensor("wv_sb", [128, 8 * D], BF16) as wv_sb,
        nc.sbuf_tensor("k_tm", [128, D], BF16) as k_tm,
        nc.sbuf_tensor("v_tm", [128, D], BF16) as v_tm,
        nc.sbuf_tensor("kv_f", [128, 512], F32) as kv_f,
        nc.sbuf_tensor("kv_acc", [128, 1024], F32) as kv_acc,
        nc.sbuf_tensor("sign_f", [128, 512], F32) as sign_f,
        nc.sbuf_tensor("bkr_bf", [1, D], BF16) as bkr_bf,
        nc.sbuf_tensor("bvr_bf", [1, D], BF16) as bvr_bf,
        nc.sbuf_tensor("cstr", [1, 2048], F32) as cstr,
    ):
        def cast_proj(dst_sb, src):
            return em.dma("gpsimd", lambda e, dst_sb=dst_sb, src=src: e.dma_start(
                out=dst_sb.ap().rearrange("p (c d) -> p c d", d=D),
                in_=src.rearrange("(c p) d -> p c d", p=128)), [], "dC")

        if 'A' in parts:
            tk_wk = cast_proj(wk_sb, wk_d)
            tk_wv = cast_proj(wv_sb, wv_d)
            tk_wq = cast_proj(wq_sb, wq_d)
            tk_wu = cast_proj(wu_sb, wu_d)
            tk_wo = cast_proj(wo_sb, wo_d)
            tk_w1 = em.dma("gpsimd", lambda e: e.dma_start(out=w1c, in_=w1tb), [], "dC")
            tk_w2 = em.dma("gpsimd", lambda e: e.dma_start(out=w2c, in_=w2tb), [], "dC")
        else:
            tk_wk = tk_wv = tk_wq = tk_wu = tk_wo = tk_w1 = tk_w2 = None

        em.dma("sync", lambda e: e.dma_start(out=cstf[:], in_=cstf_d), [], "dSm")
        tk_cst = em.dma("sync", lambda e: e.dma_start(out=cstb[:], in_=cstb_d),
                        [], "dSm")
        tk_cr = em.dma("sync", lambda e: e.dma_start(out=cstr[:], in_=cstr_d),
                       [], "dK")
        tks["cst"] = tk_cst
        tks["oc"] = em.op("vector", lambda e: e.memset(ones_c[:], 1.0), [], "sV")
        tks["or"] = em.op("vector", lambda e: e.memset(ones_r[:], 1.0), [], "sV")
        tks["ob"] = em.op("vector", lambda e: e.memset(ones_bsub[:], 1.0), [], "sV")
        em.op("scalar", lambda e: e.copy(bkr_bf[0:1, :], cstr[0:1, 0:1024]),
              [tk_cr, tk_cst], "sA")
        tk_bias = em.op("scalar", lambda e: e.copy(bvr_bf[0:1, :],
                                                   cstr[0:1, 1024:2048]), [], "sA")
        tkva = em.op("vector", lambda e: e.memset(kv_acc[:], 0.0), [tk_cst], "sV")
        em.op("vector", lambda e: e.memset(kvbd[:], 0.0), [], "sV")


        for t in range(p1_tiles):
            xw = bufs["x"].write_waits()
            tx = em.dma("sync", lambda e, t=t: e.dma_start(
                out=x_sb.ap().rearrange("p (c s) -> p c s", s=T),
                in_=xT.rearrange("(c p) (t s) -> p t c s", p=128, s=T)[:, t, :, :]),
                xw, "dX")
            bufs["x"].wrote(tx)

            ap_toks = ln_full(xf, [tx] * NCK, lnv, bufs["ln"].write_waits())
            for c in range(NCK):
                bufs["x"].read(ap_toks[c])
            bufs["ln"].wrote(ap_toks[-1])
            tst = em.dma("sync", lambda e, t=t: e.dma_start(
                out=lnx[t], in_=ln_sh[:, :]), [ap_toks[-1]], "dLs")
            bufs["ln"].read(tst)

            for u in range(NSUB):
                evs = {}
                for (w_sb, brow, dst, dst_name, do_relu, wtok) in (
                        (wk_sb, bkr_bf, k_tm, "ktm", True, tk_wk),
                        (wv_sb, bvr_bf, v_tm, "vtm", False, tk_wv)):
                    for half in range(2):
                        b = 4 + half
                        wv0 = pbank[b].write_waits() + [wtok]
                        if u == 0:
                            wv0.append(ap_toks[0])
                        for k in range(NCK):
                            wk_ = (wv0 if k == 0 else
                                   ([ap_toks[k]] if u == 0 else []))
                            em.op("tensor",
                                  lambda e, u=u, k=k, half=half, b=b, w_sb=w_sb: e.matmul(
                                      bank(b),
                                      ln_sh[:, k * T + u * 128:k * T + (u + 1) * 128],
                                      w_sb[:, k * D + half * 512:k * D + (half + 1) * 512],
                                      start=(k == 0), stop=False,
                                      skip_group_check=True),
                                  wk_)
                        em.op("tensor", lambda e, b=b, half=half, brow=brow: e.matmul(
                            bank(b), ones_bsub[:],
                            brow[0:1, half * 512:(half + 1) * 512],
                            start=False, stop=True, skip_group_check=True),
                            [tk_bias, tks["ob"]])
                        tp = em.retro_inc("tensor", "sP")
                        pbank[b].wrote(tp)
                        wv1 = [tp] + bufs[dst_name].write_waits()
                        if do_relu:
                            tev = em.op("vector",
                                        lambda e, half=half, b=b, dst=dst: e.tensor_scalar_max(
                                            dst[:, half * 512:(half + 1) * 512],
                                            bank(b), 0.0), wv1, "sV")
                        else:
                            tev = em.op("scalar",
                                        lambda e, half=half, b=b, dst=dst: e.copy(
                                            dst[:, half * 512:(half + 1) * 512],
                                            bank(b)), wv1, "sA")
                        pbank[b].read(tev)
                        bufs[dst_name].wrote(tev)
                        evs[(dst_name, half)] = tev
                for b, h in ((6, 0), (7, 1)):
                    for j in range(4):
                        p = h * 4 + j
                        slot = j * 128
                        wv = ([evs[("ktm", h)], evs[("vtm", h)], tkva]
                              + pbank[b].write_waits()) if j == 0 else []
                        em.op("tensor",
                              lambda e, p=p, b=b, slot=slot: e.matmul(
                                  bank(b, 0, 128, slot, 128),
                                  k_tm[:, p * 128:(p + 1) * 128],
                                  v_tm[:, p * 128:(p + 1) * 128],
                                  start=True, stop=True, skip_group_check=True),
                              wv)
                    tkv = em.retro_inc("tensor", "sP")
                    bufs["ktm"].read(tkv)
                    bufs["vtm"].read(tkv)
                    pbank[b].wrote(tkv)
                    tka = em.op("vector", lambda e, b=b, h=h: e.tensor_add(
                        kv_acc[:, h * 512:(h + 1) * 512],
                        kv_acc[:, h * 512:(h + 1) * 512], bank(b)), [tkv], "sV")
                    pbank[b].read(tka)

        pe_p1_end = em.tok("sP")

        for p in (range(8) if 'D' in parts else []):
            off = (p // 4) * 512 + (p % 4) * 128
            em.op("vector", lambda e, p=p, off=off: e.tensor_copy(
                kv_f[0:64, p * 64:p * 64 + 64],
                kv_acc[0:64, off:off + 64]), [])
            em.op("vector", lambda e, p=p, off=off: e.tensor_copy(
                kv_f[64:128, p * 64:p * 64 + 64],
                kv_acc[64:128, off + 64:off + 128]), [])
        if 'D' in parts:
            tcp = em.op("vector", lambda e: e.engine_nop(), [], "sV")
            tsg = em.op("scalar", lambda e: e.activation(sign_f[:], kv_f[:], AF.Sign),
                        [tcp], "sA")
            tab = em.op("scalar", lambda e: e.activation(kv_f[:], kv_f[:], AF.Abs),
                        [], "sA")
            em.op("vector", lambda e: e.tensor_scalar_max(kv_f[:], kv_f[:], 0.01), [tab])
            em.op("vector", lambda e: e.tensor_scalar_min(kv_f[:], kv_f[:], 100.0), [])
            em.op("vector", lambda e: e.tensor_mul(kv_f[:], kv_f[:], sign_f[:]), [])
            tclamp = None
            for p in range(8):
                em.op("vector", lambda e, p=p: e.tensor_copy(
                    kvbd[0:64, p * 128:p * 128 + 64],
                    kv_f[0:64, p * 64:p * 64 + 64]), [])
                tclamp = em.op("vector", lambda e, p=p: e.tensor_copy(
                    kvbd[64:128, p * 128 + 64:p * 128 + 128],
                    kv_f[64:128, p * 64:p * 64 + 64]), [], "sV")
        else:
            tclamp = None

        if dbg:
            em.dma("gpsimd", lambda e: e.dma_start(out=dbg_out["dbg_kv"], in_=kv_f[:]),
                   [tclamp], "dDbg")

        scopeA_done = [tclamp, em.tok("sP"), em.tok("sV"), em.tok("sA"),
                       em.tok("dDbg")]
        em.flush("blk1")

    import contextlib
    with contextlib.ExitStack() as _st:
        def SB(name, shape, dt):
            return _st.enter_context(nc.sbuf_tensor(name, shape, dt))
        h_sb = SB("h_sb", [128, NHC * T], BF16)
        w1blk = SB("w1blk", [128, 2 * 4096], BF16)
        w2blk = SB("w2blk", [128, 2 * 4096], BF16)
        that4 = SB("that4", [4, 4096], BF16)
        ehat = SB("ehat", [16, 1024], BF16)
        ohat = SB("ohat", [128, 512], BF16)
        odd1g = SB("odd1g", [128, 32], BF16)
        row128 = SB("row128", [1, 128], BF16)
        t_sb = SB("t_sb", [4, T], BF16)
        u2_sb = SB("u2_sb", [16, T], BF16)
        outc = SB("outc", [128, 2 * T], F32)
        o4 = SB("o4", [4, 1024], BF16)
        ohT = SB("ohT", [16, 512], BF16)
        stage2 = SB("stage2", [128, 2048], BF16)
        e2rows = [h_sb[0:1, m * 1024:(m + 1) * 1024] for m in range(4)]

        p2x = {}
        p2ln = {}
        fw1_tok = {}
        fw2_tok = {}

        def emit_ln1_load(t, eng, extra):
            tk = em.dma(eng, lambda e, t=t: e.dma_start(
                out=ln_sh[:, :], in_=lnx[t]),
                bufs["ln"].write_waits() + extra,
                "dLn" if eng == "sync" else "dLg")
            bufs["ln"].wrote(tk)
            p2ln[t] = tk

        def emit_x_load(t, eng):
            tk = em.dma(eng, lambda e, t=t: e.dma_start(
                out=x_sb.ap().rearrange("p (c s) -> p c s", s=T),
                in_=xT.rearrange("(c p) (t s) -> p t c s", p=128, s=T)[:, t, :, :]),
                bufs["x"].write_waits(), "dX" if eng == "sync" else "dXg")
            bufs["x"].wrote(tk)
            p2x[t] = tk

        for t in range(p2_tiles):
            if t == 0:
                emit_ln1_load(0, "sync", [em.tok("dLs"), pe_p1_end])
                emit_x_load(0, "sync")
            tln1 = p2ln[t]
            tx = p2x[t]

            q_toks = []
            for j in range(NCK):
                b = 4 + (j % 2)
                wv = pbank[b].write_waits() + [tk_wq, tln1]
                for k in range(NCK):
                    em.op("tensor", lambda e, j=j, k=k, b=b: e.matmul(
                        bank(b), wq_sb[:, k * D + j * 128:k * D + (j + 1) * 128],
                        lnv(k), start=(k == 0), stop=(k == NCK - 1),
                        skip_group_check=True),
                        wv if k == 0 else [])
                tp = em.retro_inc("tensor", "sP")
                pbank[b].wrote(tp)
                tq = em.op("scalar", lambda e, j=j, b=b: e.activation(
                    q_sh[:, j * T:(j + 1) * T], bank(b), AF.Relu,
                    bias=bqs[:, j:j + 1]), [tp], "sA")
                pbank[b].read(tq)
                q_toks.append(tq)
            bufs["q"].wrote(q_toks[-1])
            if dbg and t == 0:
                bufs["q"].read(em.dma("gpsimd", lambda e: e.dma_start(
                    out=dbg_out["dbg_q"].rearrange("p (c s) -> p c s", s=128),
                    in_=q_sh.ap().rearrange("p (c s) -> p c s", s=T)[:, :, 0:128]),
                    [q_toks[-1]], "dDbg"))

            attn_toks = []
            xbw = bufs["xb"].write_waits()
            for j in range(NCK):
                b = 6 + (j % 2)
                wv = pbank[b].write_waits() + [tclamp, q_toks[j]]
                em.op("tensor", lambda e, j=j, b=b: e.matmul(
                    bank(b), kvbd[:, j * 128:(j + 1) * 128],
                    q_sh[:, j * T:(j + 1) * T],
                    start=True, stop=True, skip_group_check=True), wv)
                tp = em.retro_inc("tensor", "sP")
                pbank[b].wrote(tp)
                tev = em.op("scalar", lambda e, j=j, b=b: e.copy(
                    xbv(j), bank(b)), [tp] + (xbw if j == 0 else []), "sA")
                pbank[b].read(tev)
                attn_toks.append(tev)
            bufs["xb"].wrote(attn_toks[-1])
            if dbg and t == 0:
                bufs["xb"].read(em.dma("gpsimd", lambda e: e.dma_start(
                    out=dbg_out["dbg_attn"].rearrange("p (c s) -> p c s", s=128),
                    in_=xb_sh.ap().rearrange("p (c s) -> p c s", s=T)[:, :, 0:128]),
                    [attn_toks[-1]], "dDbg"))

            if p2_level < 2:
                continue
            tp2 = ln_stats(lambda c: attn_toks[c])
            tb2m, emit_b2 = ln_rows_bcast(tp2, defer_rstd=True)
            tb2r = None

            prod_toks = []
            for c in range(NCK):
                b = 4 + (c % 4)
                wv = pbank[b].write_waits() + [tk_wu]
                if c == 0:
                    wv.append(tln1)
                for k in range(NCK):
                    em.op("tensor", lambda e, c=c, k=k, b=b: e.matmul(
                        bank(b), wu_sb[:, k * D + c * 128:k * D + (c + 1) * 128],
                        lnv(k), start=(k == 0), stop=(k == NCK - 1),
                        skip_group_check=True), wv if k == 0 else [])
                tp = em.retro_inc("tensor", "sP")
                if c == 0:
                    tb2r = emit_b2()
                pbank[b].wrote(tp)
                em.op("vector", lambda e, c=c: e.tensor_sub(
                    t1_sb[:], xbv(c), bank(3)),
                    [tb2m] + bufs["t1"].write_waits())
                tv2_ = em.op("vector", lambda e: e.tensor_mul(t1_sb[:], t1_sb[:], bank(2)),
                             [tb2r], "sV")
                bufs["t1"].wrote(tv2_)
                ts2 = em.op("scalar", lambda e, c=c: e.activation(
                    t1_sb[:], t1_sb[:], AF.Identity,
                    bias=lnbs[:, c:c + 1], scale=lnws[:, c:c + 1]), [tv2_], "sA")
                tpr = em.op("vector", lambda e, c=c, b=b: e.scalar_tensor_tensor(
                    q_sh[:, c * T:(c + 1) * T], bank(b), bus[:, c:c + 1], t1_sb[:],
                    ALU.add, ALU.mult),
                    [tp, ts2] + (bufs["q"].write_waits() if c == 0 else []), "sV")
                bufs["t1"].wrote(tpr)
                pbank[b].read(tpr)
                pbank[2].read(tpr)
                pbank[3].read(tpr)
                prod_toks.append(tpr)
            bufs["q"].wrote(prod_toks[-1])

            x1w = bufs["x1"].write_waits()
            x1_toks = []
            for j in range(NCK):
                b = 6 + (j % 2)
                wv = pbank[b].write_waits() + [tk_wo]
                for k in range(NCK):
                    em.op("tensor", lambda e, j=j, k=k, b=b: e.matmul(
                        bank(b), wo_sb[:, k * D + j * 128:k * D + (j + 1) * 128],
                        q_sh[:, k * T:(k + 1) * T], start=(k == 0),
                        stop=(k == NCK - 1), skip_group_check=True),
                        (wv + [prod_toks[k]]) if k == 0 or j == 0 else [])
                tp = em.retro_inc("tensor", "sP")
                pbank[b].wrote(tp)
                tev = em.op("vector", lambda e, j=j, b=b: e.scalar_tensor_tensor(
                    x1f(j), bank(b), bos[:, j:j + 1], xf(j), ALU.add, ALU.add),
                    [tp, tx] + (x1w if j == 0 else []), "sV")
                pbank[b].read(tev)
                bufs["x"].read(tev)
                x1_toks.append(tev)
            bufs["x1"].wrote(x1_toks[-1])
            if t + 1 < p2_tiles:
                emit_x_load(t + 1, "gpsimd")
            if dbg and t == 0:
                bufs["x1"].read(em.dma("gpsimd", lambda e: e.dma_start(
                    out=dbg_out["dbg_x1"].rearrange("p (c s) -> p c s", s=128),
                    in_=x1_sb.ap().rearrange("p (c s) -> p c s", s=T)[:, :, 0:128]),
                    [x1_toks[-1]], "dDbg"))

            if p2_level < 3:
                continue
            ln_war = em.tok("sP")
            ap3 = ln_full(x1f, x1_toks, lnv,
                          bufs["ln"].write_waits() + [ln_war])
            for c in range(NCK):
                bufs["x1"].read(ap3[c])
            bufs["ln"].wrote(ap3[-1])
            if dbg and t == 0:
                bufs["ln"].read(em.dma("gpsimd", lambda e: e.dma_start(
                    out=dbg_out["dbg_ln3"].rearrange("p (c s) -> p c s", s=128),
                    in_=ln_sh.ap().rearrange("p (c s) -> p c s", s=T)[:, :, 0:128]),
                    [ap3[-1]], "dDbg"))

            if t == 0:
                for k in range(NCK):
                    em.op("vector", lambda e, k=k: e.tensor_copy(
                        odd1g[:, k * 4:(k + 1) * 4],
                        ln_sh.ap()[:, k * T + 2:k * T + 9:2]),
                        [ap3[-1]] if k == 0 else [])
                em.op("vector", lambda e: e.memset(ohat[:], 0.0), [])
                for jc in range(NHC):
                    r, k = jc // 8, jc % 8
                    em.op("vector", lambda e, jc=jc, r=r, k=k: e.tensor_copy(
                        ohat[:, jc * 16 + 4 * r:jc * 16 + 4 * r + 4],
                        ln_sh.ap()[:, k * T + 10:k * T + 17:2]), [], "sV")
                for c in range(NHC):
                    i, k = c // 8, c % 8
                    wv = pbank[4].write_waits()
                    tp1 = em.op("tensor", lambda e, i=i, k=k: e.transpose(
                        bank(4, 0, 1, 0, 64).bitcast(BF16),
                        ln_sh[:, k * T + 1 + 2 * i:k * T + 2 + 2 * i], ident[:]),
                        wv + [ap3[-1], tks["cst"]], "sP")
                    pbank[4].wrote(tp1)
                    tv1 = em.op("vector", lambda e: e.tensor_scalar_mul(
                        row128[:], bank(4, 0, 1, 0, 64).bitcast(BF16),
                        scale_s[0:1, 0:1]), [tp1], "sV")
                    pbank[4].read(tv1)
                    wv = pbank[5].write_waits()
                    tp2_ = em.op("tensor", lambda e, i=i: e.matmul(
                        bank(5, 0, 4, 0, 128), oh4[0:1, i * 4:(i + 1) * 4],
                        row128[:], start=True, stop=True, skip_group_check=True),
                        wv + [tv1], "sP")
                    pbank[5].wrote(tp2_)
                    tv2_ = em.op("vector", lambda e, c=c: e.tensor_copy(
                        that4[0:4, c * 128:(c + 1) * 128], bank(5, 0, 4, 0, 128)),
                        [tp2_], "sV")
                    pbank[5].read(tv2_)
                for m in range(4):
                    for k in range(NCK):
                        wv = pbank[4].write_waits()
                        tp1 = em.op("tensor", lambda e, m=m, k=k: e.transpose(
                            bank(4, 0, 1, 0, 64).bitcast(BF16),
                            ln_sh[:, k * T + 9 + 2 * m:k * T + 10 + 2 * m],
                            ident[:]), wv, "sP")
                        pbank[4].wrote(tp1)
                        tv1 = em.op("vector", lambda e, m=m, k=k: e.tensor_copy(
                            e2rows[m][0:1, k * 128:(k + 1) * 128],
                            bank(4, 0, 1, 0, 64).bitcast(BF16)), [tp1], "sV")
                        pbank[4].read(tv1)
                te2 = None
                for m in range(4):
                    te2 = em.op("vector", lambda e, m=m: e.tensor_scalar_mul(
                        e2rows[m][:], e2rows[m][:], scale_s[0:1, 0:1]), [], "sV")
                for c in range(NCK):
                    m = c // 2
                    wv = pbank[5].write_waits()
                    for r in range(4):
                        rho = 4 * r + m
                        st = 512 * (c % 2) + r
                        em.op("tensor",
                              lambda e, m=m, rho=rho, st=st, r=r: e.matmul(
                                  bank(5, 0, 16, 0, 128),
                                  oh16[0:1, rho * 16:(rho + 1) * 16],
                                  e2rows[m][0:1, st:st + 509:4],
                                  start=(r == 0), stop=(r == 3),
                                  skip_group_check=True),
                              (wv + [te2]) if r == 0 else [])
                    tp = em.retro_inc("tensor", "sP")
                    pbank[5].wrote(tp)
                    tv = em.op("vector", lambda e, c=c: e.tensor_copy(
                        ehat[0:16, c * 128:(c + 1) * 128], bank(5, 0, 16, 0, 128)),
                        [tp], "sV")
                    pbank[5].read(tv)
                bufs["t_sb"].wrote(tv)
                bufs["h"].read(tp)
                t_o1g = em.tok("sV")
                for k in range(NCK):
                    wv = pbank[4].write_waits()
                    tp1 = em.op("tensor", lambda e, k=k: e.transpose(
                        bank(4, 0, 4, 0, 64).bitcast(BF16),
                        odd1g[:, k * 4:(k + 1) * 4], ident[:]),
                        wv + [t_o1g], "sP")
                    pbank[4].wrote(tp1)
                    tv1 = em.op("vector", lambda e, k=k: e.tensor_copy(
                        o4[0:4, k * 128:(k + 1) * 128],
                        bank(4, 0, 4, 0, 64).bitcast(BF16)), [tp1], "sV")
                    pbank[4].read(tv1)
                t_mach = em.tok("sV")

            if t == 0:
                wv = pbank[0].write_waits() + [bufs["t_sb"].writer, ap3[0]]
                for k in range(NCK):
                    em.op("tensor", lambda e, k=k: e.matmul(
                        bank(0, 0, 4), odd1g[:, k * 4:(k + 1) * 4], lnv(k),
                        start=(k == 0), stop=(k == NCK - 1),
                        skip_group_check=True),
                        wv if k == 0 else [ap3[k]])
                tpt = em.retro_inc("tensor", "sP")
                pbank[0].wrote(tpt)
                ttv = em.op("vector", lambda e: e.tensor_copy(t_sb[:], bank(0, 0, 4)),
                            [tpt] + bufs["t_sb"].write_waits(), "sV")
                pbank[0].read(ttv)
                bufs["t_sb"].wrote(ttv)
                if dbg:
                    bufs["t_sb"].read(em.dma("gpsimd", lambda e: e.dma_start(
                        out=dbg_out["dbg_t"], in_=t_sb[:]), [ttv], "dDbg"))

            if p2_level < 4:
                continue
            hw = bufs["h"].write_waits()
            u2w = pbank[1].write_waits() if t == 0 else []
            h_toks = []
            w1_toks = [None] * 8

            def load_w1(jcb):
                wbuf = jcb % 2
                bname = "w1b%d" % wbuf
                src = w1c if t == 0 else w1f
                extra = [tk_w1] if t == 0 else [fw1_tok[jcb]]
                tw = em.dma("sync", lambda e, jcb=jcb, wbuf=wbuf, src=src: e.dma_start(
                    out=w1blk[:, wbuf * 4096:(wbuf + 1) * 4096], in_=src[jcb]),
                    bufs[bname].write_waits() + extra
                    + (scopeA_done if t == 0 and jcb < 2 else []),
                    "dW1a" if wbuf == 0 else "dW1b")
                w1_toks[jcb] = tw
                bufs[bname].wrote(tw)

            def fold_w1(jcb):
                wbuf = jcb % 2
                bname = "w1b%d" % wbuf
                rd = bufs[bname].write_waits()
                ta = None
                for k in range(NCK):
                    bf = 2 + (k % 2)
                    wvf = pbank[bf].write_waits() + ([t_mach] if k < 2 else [])
                    tm = em.op("tensor", lambda e, k=k, bf=bf, jcb=jcb: e.matmul(
                        bank(bf), o4[0:4, k * 128:(k + 1) * 128],
                        that4[0:4, jcb * 512:(jcb + 1) * 512],
                        start=True, stop=True, skip_group_check=True), wvf, "sP")
                    pbank[bf].wrote(tm)
                    ta = em.op("vector", lambda e, k=k, bf=bf, wbuf=wbuf: e.tensor_add(
                        w1blk[:, wbuf * 4096 + k * 512:wbuf * 4096 + (k + 1) * 512],
                        w1blk[:, wbuf * 4096 + k * 512:wbuf * 4096 + (k + 1) * 512],
                        bank(bf)), [tm] + (rd if k == 0 else []), "sV")
                    pbank[bf].read(ta)
                ts = em.dma("sync", lambda e, jcb=jcb, wbuf=wbuf: e.dma_start(
                    out=w1f[jcb], in_=w1blk[:, wbuf * 4096:(wbuf + 1) * 4096]),
                    [ta], "dF1")
                bufs[bname].read(ts)
                fw1_tok[jcb] = ts

            def fold_w2(jb):
                ohw = bufs["oht"].write_waits()
                cps = []
                for jsub in range(4):
                    jc = jb * 4 + jsub
                    bf = 2 + (jsub % 2)
                    wvf = pbank[bf].write_waits() + ([t_mach] if jsub < 2 else [])
                    tp1 = em.op("tensor", lambda e, jc=jc, bf=bf: e.transpose(
                        bank(bf, 0, 16, 0, 64).bitcast(BF16),
                        ohat[:, jc * 16:(jc + 1) * 16], ident[:]), wvf, "sP")
                    pbank[bf].wrote(tp1)
                    tcp = em.op("vector", lambda e, jsub=jsub, bf=bf: e.tensor_copy(
                        ohT[0:16, jsub * 128:(jsub + 1) * 128],
                        bank(bf, 0, 16, 0, 64).bitcast(BF16)),
                        [tp1] + (ohw if jsub == 0 else []), "sV")
                    pbank[bf].read(tcp)
                    cps.append(tcp)
                tls = {}

                def stage_load(jsub):
                    hb = jsub % 2
                    tls[jsub] = em.dma("gpsimd", lambda e, jb=jb, jsub=jsub, hb=hb:
                                       e.dma_start(
                                           out=stage2[:, hb * 1024:(hb + 1) * 1024],
                                           in_=w2c[jb][:, jsub * 1024:(jsub + 1) * 1024]),
                                       bufs["stg%d" % hb].write_waits() + [tk_w2],
                                       "dF2s")

                stage_load(0)
                stage_load(1)
                tm = None
                for jsub in range(4):
                    hb = jsub % 2
                    ta = None
                    for hh in range(2):
                        bf = (0, 1, 6, 7)[(2 * jsub + hh) % 4]
                        wvf = pbank[bf].write_waits() + [cps[jsub]]
                        tm = em.op("tensor",
                                   lambda e, jsub=jsub, bf=bf, hh=hh: e.matmul(
                                       bank(bf),
                                       ohT[0:16, jsub * 128:(jsub + 1) * 128],
                                       ehat[0:16, hh * 512:(hh + 1) * 512],
                                       start=True, stop=True,
                                       skip_group_check=True), wvf, "sP")
                        pbank[bf].wrote(tm)
                        ta = em.op("vector",
                                   lambda e, hb=hb, hh=hh, bf=bf: e.tensor_add(
                                       stage2[:, hb * 1024 + hh * 512:
                                              hb * 1024 + (hh + 1) * 512],
                                       stage2[:, hb * 1024 + hh * 512:
                                              hb * 1024 + (hh + 1) * 512],
                                       bank(bf)), [tm, tls[jsub]], "sV")
                        pbank[bf].read(ta)
                    ts = em.dma("sync", lambda e, jb=jb, jsub=jsub, hb=hb:
                                e.dma_start(
                                    out=w2f[jb][:, jsub * 1024:(jsub + 1) * 1024],
                                    in_=stage2[:, hb * 1024:(hb + 1) * 1024]),
                                [ta], "dF2")
                    bufs["stg%d" % hb].read(ts)
                    if jsub + 2 < 4:
                        stage_load(jsub + 2)
                bufs["oht"].read(tm)
                fw2_tok[jb] = ts

            load_w1(0)
            load_w1(1)
            for jcb in range(8):
                wbuf = jcb % 2
                bname = "w1b%d" % wbuf
                for js in range(4):
                    jc = jcb * 4 + js
                    b = 4 + (jc % 2)
                    wv = pbank[b].write_waits() + [w1_toks[jcb], ap3[0]]
                    if t == 0:
                        wv.append(bufs["t_sb"].writer)
                    for k in range(NCK):
                        em.op("tensor",
                              lambda e, js=js, k=k, b=b, wbuf=wbuf, t=t: e.matmul(
                                  bank(b),
                                  w1blk[:, wbuf * 4096 + k * 512 + js * 128:
                                        wbuf * 4096 + k * 512 + (js + 1) * 128],
                                  lnv(k), start=(k == 0),
                                  stop=(t > 0 and k == NCK - 1),
                                  skip_group_check=True),
                              wv if k == 0 else [ap3[k]])
                    if t == 0:
                        em.op("tensor", lambda e, jc=jc, b=b: e.matmul(
                            bank(b), that4[0:4, jc * 128:(jc + 1) * 128], t_sb[:],
                            start=False, stop=True, skip_group_check=True), [])
                    tp = em.retro_inc("tensor", "sP")
                    pbank[b].wrote(tp)
                    bufs[bname].read(tp)
                    if t == 0:
                        bufs["t_sb"].read(tp)
                    th = em.op("scalar", lambda e, jc=jc, b=b: e.activation(
                        h_sb[:, jc * T:(jc + 1) * T], bank(b), AF.Gelu,
                        bias=b1s[:, jc:jc + 1]),
                        [tp] + (hw if jc == 0 else []), "sA")
                    pbank[b].read(th)
                    h_toks.append(th)
                if t == 0 and p2_level >= 5:
                    fold_w1(jcb)
                    fold_w2(jcb)
                if jcb + 2 < 8:
                    load_w1(jcb + 2)
            bufs["h"].wrote(h_toks[-1])
            bufs["ln"].read(em.tok("sP"))
            if t + 1 < p2_tiles:
                emit_ln1_load(t + 1, "gpsimd", [])
            if t == 0:
                u2w = u2w + pbank[1].write_waits()
                for jc in range(NHC):
                    em.op("tensor", lambda e, jc=jc: e.matmul(
                        bank(1, 0, 16), ohat[:, jc * 16:(jc + 1) * 16],
                        h_sb[:, jc * T:(jc + 1) * T],
                        start=(jc == 0), stop=(jc == NHC - 1),
                        skip_group_check=True),
                        ([h_toks[-1], bufs["t_sb"].writer] + u2w) if jc == 0 else [])
                tpu2 = em.retro_inc("tensor", "sP")
                pbank[1].wrote(tpu2)
                tu2 = em.op("vector", lambda e: e.tensor_copy(u2_sb[:], bank(1, 0, 16)),
                            [tpu2] + bufs["u2_sb"].write_waits(), "sV")
                pbank[1].read(tu2)
                bufs["u2_sb"].wrote(tu2)
            if dbg and t == 0:
                bufs["u2_sb"].read(em.dma("gpsimd", lambda e: e.dma_start(
                    out=dbg_out["dbg_u2"], in_=u2_sb[:]), [tu2], "dDbg"))
                bufs["h"].read(em.dma("gpsimd", lambda e: e.dma_start(
                    out=dbg_out["dbg_h"], in_=h_sb[:, 0:512]), [h_toks[-1]], "dDbg"))

            if p2_level < 5:
                continue
            w2_toks = [None] * 8

            def load_w2(jb):
                wbuf = jb % 2
                bname = "w2b%d" % wbuf
                src = w2c if t == 0 else w2f
                extra = [tk_w2] if t == 0 else [fw2_tok[jb]]
                tw = em.dma("sync", lambda e, jb=jb, wbuf=wbuf, src=src: e.dma_start(
                    out=w2blk[:, wbuf * 4096:(wbuf + 1) * 4096], in_=src[jb]),
                    bufs[bname].write_waits() + extra
                    + (scopeA_done if t == 0 and jb < 2 else []),
                    "dW2a" if wbuf == 0 else "dW2b")
                w2_toks[jb] = tw
                bufs[bname].wrote(tw)

            load_w2(0)
            load_w2(1)
            bank_wv = []
            for b in range(8):
                bank_wv += pbank[b].write_waits()
            for jb in range(8):
                wbuf = jb % 2
                bname = "w2b%d" % wbuf
                for jsub in range(4):
                    jc = jb * 4 + jsub
                    for c in range(8):
                        if jc == 0 and c == 0:
                            wv = bank_wv + [w2_toks[jb], h_toks[0]]
                        elif c == 0:
                            wv = [w2_toks[jb], h_toks[jc]]
                        else:
                            wv = []
                        em.op("tensor",
                              lambda e, jsub=jsub, c=c, jc=jc, wbuf=wbuf, t=t: e.matmul(
                                  bank(c),
                                  w2blk[:, wbuf * 4096 + jsub * 1024 + c * 128:
                                        wbuf * 4096 + jsub * 1024 + (c + 1) * 128],
                                  h_sb[:, jc * T:(jc + 1) * T],
                                  start=(jc == 0),
                                  stop=(t > 0 and jc == NHC - 1),
                                  skip_group_check=True), wv)
                    tp = em.retro_inc("tensor", "sP")
                    bufs[bname].read(tp)
                    bufs["h"].read(tp)
                if jb + 2 < 8:
                    load_w2(jb + 2)
            tp_last = tp
            for c in (4, 5, 6, 7, 0, 1, 2, 3):
                if t == 0:
                    em.op("tensor", lambda e, c=c: e.matmul(
                        bank(c), ehat[0:16, c * 128:(c + 1) * 128], u2_sb[:],
                        start=False, stop=True, skip_group_check=True),
                        [tu2] if c == 4 else [])
                    tp = em.retro_inc("tensor", "sP")
                    bufs["u2_sb"].read(tp)
                else:
                    tp = tp_last
                pbank[c].wrote(tp)
                ob = c % 2
                obname = "outc%d" % ob
                tev = em.op("vector", lambda e, c=c, ob=ob: e.scalar_tensor_tensor(
                    outc[:, ob * T:(ob + 1) * T], bank(c), b2s[:, c:c + 1],
                    x1f(c), ALU.add, ALU.add),
                    [tp, ap3[-1]] + bufs[obname].write_waits(), "sV")
                pbank[c].read(tev)
                bufs["x1"].read(tev)
                tdma = em.dma("gpsimd", lambda e, c=c, t=t, ob=ob: e.dma_start(
                    out=outT.rearrange("(c p) (t s) -> p c t s", p=128, s=T)[:, c, t, :],
                    in_=outc[:, ob * T:(ob + 1) * T]), [tev],
                    "dOa" if ob == 0 else "dOb")
                bufs[obname].wrote(tev)
                bufs[obname].read(tdma)

        em.op("gpsimd", lambda e: e.engine_nop(),
              [em.tok("dOa"), em.tok("dOb"), em.tok("dDbg")])
        em.flush("blk2")

    return nc


def _host_layout(inputs):
    q = inputs
    xs = np.ascontiguousarray(np.asarray(q["L0_attn_query"], dtype=np.float32))
    W1 = np.asarray(q["L0_ffn_W1"], dtype=np.float32)
    W2 = np.asarray(q["L0_ffn_W2"], dtype=np.float32)
    w1t = np.ascontiguousarray(W1.T)
    w2t = np.ascontiguousarray(W2.T)
    w1tb = np.ascontiguousarray(
        w1t.reshape(8, 128, 8, 512).transpose(2, 1, 0, 3)).reshape(8, 128, 4096)
    w2tb = np.ascontiguousarray(
        w2t.reshape(8, 4, 128, 1024).transpose(0, 2, 1, 3)).reshape(8, 128, 4096)

    def pp(v, n):
        return np.ascontiguousarray(np.asarray(v, dtype=np.float32).reshape(n, 128).T)

    cst_f = np.zeros((128, 83), dtype=np.float32)
    cst_f[:, 0:8] = pp(q["L0_attn_bq"], 8)
    cst_f[:, 8:16] = pp(q["L0_attn_bu"], 8)
    cst_f[:, 16:24] = pp(q["L0_attn_bo"], 8)
    cst_f[:, 24:32] = pp(q["L0_ffn_b2"], 8)
    cst_f[:, 32:40] = pp(q["L0_attn_ln_w"], 8)
    cst_f[:, 40:48] = pp(q["L0_attn_ln_b"], 8)
    cst_f[:, 48:80] = pp(q["L0_ffn_b1"], 32)
    cst_f[0, 80] = np.asarray(q["L0_ffn_scale"], dtype=np.float32).reshape(())
    cst_f[0, 81] = 1e-5
    cst_f[0, 82] = -0.5

    cst_b = np.zeros((128, 401), dtype=ml_dtypes.bfloat16)
    cst_b[:, 0:128] = np.eye(128, dtype=ml_dtypes.bfloat16)
    for rho in range(16):
        cst_b[0, 128 + rho * 16 + rho] = 1
    for i in range(4):
        cst_b[0, 384 + i * 4 + i] = 1

    cst_r = np.zeros((1, 2048), dtype=np.float32)
    cst_r[0, 0:1024] = np.asarray(q["L0_attn_bk"], dtype=np.float32)
    cst_r[0, 1024:2048] = np.asarray(q["L0_attn_bv"], dtype=np.float32)

    common = {
        "wq": np.ascontiguousarray(np.asarray(q["L0_attn_Wq"], dtype=np.float32)),
        "wk": np.ascontiguousarray(np.asarray(q["L0_attn_Wk"], dtype=np.float32)),
        "wv": np.ascontiguousarray(np.asarray(q["L0_attn_Wv"], dtype=np.float32)),
        "wu": np.ascontiguousarray(np.asarray(q["L0_attn_Wu"], dtype=np.float32)),
        "wo": np.ascontiguousarray(np.asarray(q["L0_attn_Wo"], dtype=np.float32)),
        "w1tb": w1tb, "w2tb": w2tb,
        "cst_f": cst_f, "cst_b": cst_b, "cst_r": cst_r,
    }
    return [dict(common, xT=np.ascontiguousarray(xs[i].T))
            for i in range(xs.shape[0])]


def kernel(**inputs):
    in_maps = _host_layout(inputs)
    nc = build_nc(dbg=DBG)
    res = run_bass_kernel_spmd(nc, in_maps, core_ids=list(range(8)))
    outs = [np.ascontiguousarray(np.asarray(r["outT"]).T) for r in res.results]
    return np.stack(outs, axis=0)
